# version 02 (final)
# baseline (speedup 1.0000x reference)
import numpy as np
from contextlib import ExitStack
import concourse.bass as bass
import concourse.mybir as mybir
from concourse.bass_utils import run_bass_kernel_spmd

F32 = mybir.dt.float32
BF16 = mybir.dt.bfloat16
AF = mybir.ActivationFunctionType
ALU = mybir.AluOpType
AX = mybir.AxisListType

NCORES = 8
B, T, D = 4, 4096, 1024
NPC = T // NCORES
TOK = B * NPC

SECNAME = {"pe": "tensor", "act": "scalar", "dve": "vector", "pool": "gpsimd", "sp": "sync"}
ENGS = ("pe", "act", "dve", "pool", "sp")


class Prog:

    def __init__(self, nc, stk):
        self.nc = nc
        self.stk = stk
        self.csem = {e: stk.enter_context(nc.semaphore(f"c_{e}")) for e in ENGS}
        self.dsem = {e: stk.enter_context(nc.semaphore(f"d_{e}")) for e in ENGS}
        self.ops = {e: [] for e in ENGS}
        self.ccnt = {e: 0 for e in ENGS}
        self.dcnt = {e: 0 for e in ENGS}

    def _wait(self, eng, toks):
        m = {}
        for t in toks:
            if t is None:
                continue
            e, kind, c = t
            k = (e, kind)
            if c > m.get(k, 0):
                m[k] = c
        for (e, kind), c in sorted(m.items()):
            self.ops[eng].append(("w", e, kind, c))

    def emit(self, eng, fn, deps=(), dma=False):
        self._wait(eng, deps)
        if dma:
            self.dcnt[eng] += 16
            self.ops[eng].append(("o", fn, "d", 16))
            return (eng, "d", self.dcnt[eng])
        self.ccnt[eng] += 1
        self.ops[eng].append(("o", fn, "c", 1))
        return (eng, "c", self.ccnt[eng])

    def barrier(self):
        toks = [(e, "c", self.ccnt[e]) for e in ENGS if self.ccnt[e] > 0]
        toks += [(e, "d", self.dcnt[e]) for e in ENGS if self.dcnt[e] > 0]
        for e in ENGS:
            self._wait(e, [t for t in toks if t[0] != e])
        return toks

    def finalize(self):
        bend = {}
        for e in ENGS:
            ends = {}
            run = []
            cum = 0
            for item in self.ops[e]:
                if item[0] == "w":
                    for v in run:
                        ends[v] = cum
                    run = []
                elif item[2] == "d":
                    cum += item[3]
                    run.append(cum)
            for v in run:
                ends[v] = cum
            bend[e] = ends
        blk = self.stk.enter_context(self.nc.Block())
        for e in ENGS:
            ops = self.ops[e]
            csem, dsem = self.csem, self.dsem

            def body(q, e=e, ops=ops):
                cum = 0
                after_wait = False
                for item in ops:
                    if item[0] == "w":
                        _, se, kind, c = item
                        if kind == "d":
                            q.wait_ge(dsem[se], bend[se][c])
                        else:
                            q.wait_ge(csem[se], c)
                        after_wait = True
                    else:
                        if item[2] == "d":
                            if after_wait and cum > 0:
                                q.wait_ge(dsem[e], cum)
                            after_wait = False
                            inst = item[1](q)
                            inst.then_inc(dsem[e], 16)
                            cum += 16
                        else:
                            inst = item[1](q)
                            inst.then_inc(csem[e], 1)

            getattr(blk, SECNAME[e])(body)


def new_nc():
    return bass.Bass("TRN2", target_bir_lowering=False)


def ceil_div(a, b):
    return (a + b - 1) // b



def build_launch1(nc):
    H, DK, DV = 4, 256, 512
    x = nc.dram_tensor("xsh", [TOK, D], F32, kind="ExternalInput").ap()
    wq = nc.dram_tensor("wq", [D, H * DK], F32, kind="ExternalInput").ap()
    wk = nc.dram_tensor("wk", [D, H * DK], F32, kind="ExternalInput").ap()
    wv = nc.dram_tensor("wv", [D, H * DV], F32, kind="ExternalInput").ap()
    wb = nc.dram_tensor("wb", [D, H], F32, kind="ExternalInput").ap()
    lnw = nc.dram_tensor("lnw", [D], F32, kind="ExternalInput").ap()
    lnb = nc.dram_tensor("lnb", [D], F32, kind="ExternalInput").ap()
    ident = nc.dram_tensor("ident", [128, 128], F32, kind="ExternalInput").ap()
    qT = nc.dram_tensor("qT", [B, H, DK, NPC], F32, kind="ExternalOutput").ap()
    kT = nc.dram_tensor("kT", [B, H, DK, NPC], F32, kind="ExternalOutput").ap()
    vT = nc.dram_tensor("vT", [B, H, DV, NPC], F32, kind="ExternalOutput").ap()
    bet = nc.dram_tensor("bet", [B, H, NPC], F32, kind="ExternalOutput").ap()

    NT = TOK // 128
    with ExitStack() as stk:
        P = Prog(nc, stk)
        sb = lambda n, s, d: stk.enter_context(nc.sbuf_tensor(n, s, d))
        ps = lambda n, s: stk.enter_context(nc.psum_tensor(n, s, F32))

        idf = sb("idf", [128, 128], F32)
        t_id = P.emit("sp", lambda q: q.dma_start(out=idf[:, :], in_=ident[:, :]), dma=True)
        lnwb = sb("lnwb", [128, D], F32)
        lnbb = sb("lnbb", [128, D], F32)
        t_lnw = P.emit("sp", lambda q: q.dma_start(out=lnwb[:, :], in_=lnw.partition_broadcast(128)), dma=True)
        t_lnb = P.emit("sp", lambda q: q.dma_start(out=lnbb[:, :], in_=lnb.partition_broadcast(128)), dma=True)

        xnT = [sb(f"xnT{i}", [128, TOK], BF16) for i in range(8)]

        xt = sb("xt", [128, D], F32)
        xm = sb("xm", [128, D], F32)
        sqt = sb("sqt", [128, D], F32)
        nrm = sb("nrm", [128, D], F32)
        col = sb("col", [128, 1], F32)
        col2 = sb("col2", [128, 1], F32)
        ptr = ps("ptr", [128, 512])

        tok_done = []
        prev = []
        for i in range(NT):
            tl = P.emit("sp", lambda q, i=i: q.dma_start(out=xt[:, :], in_=x[i * 128:(i + 1) * 128, :]),
                        deps=prev, dma=True)
            t1 = P.emit("dve", lambda q: q.reduce_sum(col[:, :], xt[:, :], axis=AX.X), deps=[tl])
            t2 = P.emit("dve", lambda q: q.tensor_scalar_mul(col2[:, :], col[:, :], 1.0 / D), deps=[t1])
            t3 = P.emit("dve", lambda q: q.tensor_scalar_sub(xm[:, :], xt[:, :], col2[:, :]), deps=[t2])
            t4 = P.emit("act", lambda q: q.square(sqt[:, :], xm[:, :]), deps=[t3])
            t5 = P.emit("dve", lambda q: q.reduce_sum(col[:, :], sqt[:, :], axis=AX.X), deps=[t4])
            t6 = P.emit("dve", lambda q: q.tensor_scalar(col2[:, :], col[:, :], 1.0 / D, 1e-5,
                                                         ALU.mult, ALU.add), deps=[t5])
            t6b = P.emit("act", lambda q: q.sqrt(col[:, :], col2[:, :]), deps=[t6])
            t7 = P.emit("dve", lambda q: q.reciprocal(col2[:, :], col[:, :]), deps=[t6b])
            t8 = P.emit("dve", lambda q: q.scalar_tensor_tensor(
                nrm[:, :], xm[:, :], col2[:, :], lnwb[:, :], ALU.mult, ALU.mult),
                deps=[t7, t_lnw])
            t9 = P.emit("dve", lambda q: q.tensor_add(nrm[:, :], nrm[:, :], lnbb[:, :]),
                        deps=[t8, t_lnb])
            prev = []
            for j in range(8):
                tt = P.emit("pe", lambda q, j=j: q.transpose(
                    ptr[:, 0:128], nrm[:, j * 128:(j + 1) * 128], idf[:, :]),
                    deps=[t9, t_id] + prev)
                te = P.emit("act", lambda q, i=i, j=j: q.copy(
                    xnT[j][:, i * 128:(i + 1) * 128], ptr[:, 0:128]), deps=[tt])
                prev = [te]
            tok_done.append(prev[0])

        start = P.barrier()

        wstage = sb("wstage", [128, 2048], F32)
        stage_free = [None]

        def cast_weight(hbm_ap, cols, name, nrows=1024):
            bufs = [sb(f"{name}_b{e}", [128, cols], BF16) for e in range(nrows // 128)]
            toks = []
            for e in range(nrows // 128):
                tl = P.emit("sp", lambda q, e=e: q.dma_start(
                    out=wstage[:, 0:cols], in_=hbm_ap[e * 128:(e + 1) * 128, :]),
                    deps=list(start) + ([stage_free[0]] if stage_free[0] else []), dma=True)
                tc = P.emit("pool", lambda q, e=e: q.tensor_copy(
                    out=bufs[e][:, :], in_=wstage[:, 0:cols]), deps=[tl])
                stage_free[0] = tc
                toks.append(tc)
            return bufs, toks

        pbank = [ps(f"pb{t}", [128, 512]) for t in range(4)]
        prow = ps("prow", [1, 512])
        pbc = ps("pbc", [128, 512])
        pbeta = ps("pbeta", [4, 512])

        ones1 = sb("ones1", [1, 128], F32)
        onesc = sb("onesc", [128, 1], F32)
        t_o1 = P.emit("dve", lambda q: q.memset(ones1[:, :], 1.0))
        t_oc = P.emit("dve", lambda q: q.memset(onesc[:, :], 1.0))

        bank_free = {tc: None for tc in range(4)}
        slab = [sb(f"slab{u}", [128, TOK], F32) for u in range(2)]
        sqs = [sb(f"sqs{u}", [128, TOK], F32) for u in range(2)]
        rrow = sb("rrow", [1, TOK], F32)
        rrow2 = sb("rrow2", [1, TOK], F32)
        bcs = sb("bcs", [128, 512], F32)
        state = {"slab_free": [], "prow_free": None, "pbc_free": None}

        def proj_qk(wbufs, wtoks, out_hbm, scale):
            for h in range(4):
                ev = []
                for u in range(2):
                    o = h * 2 + u
                    for tc in range(4):
                        last = None
                        for e in range(8):
                            dd = []
                            if e == 0:
                                dd = wtoks + tok_done + state["slab_free"]
                                if bank_free[tc] is not None:
                                    dd = dd + [bank_free[tc]]
                            last = P.emit("pe", lambda q, e=e, tc=tc, o=o: q.matmul(
                                pbank[tc][:, :], wbufs[e][:, o * 128:(o + 1) * 128],
                                xnT[e][:, tc * 512:(tc + 1) * 512],
                                start=(e == 0), stop=(e == 7)), deps=dd)
                        t = P.emit("act", lambda q, u=u, tc=tc: q.activation(
                            slab[u][:, tc * 512:(tc + 1) * 512], pbank[tc][:, :], AF.Silu),
                            deps=[last])
                        bank_free[tc] = t
                        ev.append(t)
                ts1 = P.emit("act", lambda q: q.square(sqs[0][:, :], slab[0][:, :]), deps=ev)
                ts2 = P.emit("act", lambda q: q.square(sqs[1][:, :], slab[1][:, :]), deps=ev)
                wr = []
                for tc in range(4):
                    d0 = [ts1, ts2, t_oc]
                    if state["prow_free"] is not None:
                        d0.append(state["prow_free"])
                    P.emit("pe", lambda q, tc=tc: q.matmul(
                        prow[0:1, :], onesc[:, :], sqs[0][:, tc * 512:(tc + 1) * 512],
                        start=True, stop=False), deps=d0)
                    m2 = P.emit("pe", lambda q, tc=tc: q.matmul(
                        prow[0:1, :], onesc[:, :], sqs[1][:, tc * 512:(tc + 1) * 512],
                        start=False, stop=True), deps=[])
                    r0 = P.emit("dve", lambda q, tc=tc: q.tensor_scalar_add(
                        rrow2[0:1, tc * 512:(tc + 1) * 512], prow[0:1, :], 1e-6), deps=[m2])
                    state["prow_free"] = r0
                    r1 = P.emit("act", lambda q, tc=tc: q.sqrt(
                        rrow[0:1, tc * 512:(tc + 1) * 512], rrow2[0:1, tc * 512:(tc + 1) * 512]),
                        deps=[r0])
                    r2 = P.emit("dve", lambda q, tc=tc: q.reciprocal(
                        rrow2[0:1, tc * 512:(tc + 1) * 512], rrow[0:1, tc * 512:(tc + 1) * 512]),
                        deps=[r1])
                    d1 = [r2, t_o1]
                    if state["pbc_free"] is not None:
                        d1.append(state["pbc_free"])
                    bcm = P.emit("pe", lambda q, tc=tc: q.matmul(
                        pbc[:, :], ones1[0:1, :], rrow2[0:1, tc * 512:(tc + 1) * 512],
                        start=True, stop=True), deps=d1)
                    bce = P.emit("dve", lambda q: q.tensor_copy(out=bcs[:, :], in_=pbc[:, :]),
                                 deps=[bcm])
                    state["pbc_free"] = bce
                    for u in range(2):
                        tm = P.emit("dve", lambda q, u=u, tc=tc: q.scalar_tensor_tensor(
                            slab[u][:, tc * 512:(tc + 1) * 512],
                            slab[u][:, tc * 512:(tc + 1) * 512], scale, bcs[:, :],
                            ALU.mult, ALU.mult), deps=[bce])
                        wr.append(tm)
                state["slab_free"] = []
                for u in range(2):
                    for b in range(4):
                        tw = P.emit("pool", lambda q, h=h, u=u, b=b: q.dma_start(
                            out=out_hbm[b, h, u * 128:(u + 1) * 128, :],
                            in_=slab[u][:, b * NPC:(b + 1) * NPC]), deps=wr, dma=True)
                        state["slab_free"].append(tw)

        wqb, wqt = cast_weight(wq, 1024, "wq")
        proj_qk(wqb, wqt, qT, 1.0 / 16.0)
        wkb, wkt = cast_weight(wk, 1024, "wk")
        proj_qk(wkb, wkt, kT, 1.0)

        wvb, wvt = cast_weight(wv, 2048, "wv")
        vslab = sb("vslab", [128, TOK], F32)
        vfree = []
        for o in range(16):
            ev = []
            for tc in range(4):
                last = None
                for e in range(8):
                    dd = []
                    if e == 0:
                        dd = wvt + tok_done + vfree
                        if bank_free[tc] is not None:
                            dd = dd + [bank_free[tc]]
                    last = P.emit("pe", lambda q, e=e, tc=tc, o=o: q.matmul(
                        pbank[tc][:, :], wvb[e][:, o * 128:(o + 1) * 128],
                        xnT[e][:, tc * 512:(tc + 1) * 512],
                        start=(e == 0), stop=(e == 7)), deps=dd)
                t = P.emit("act", lambda q, tc=tc: q.activation(
                    vslab[:, tc * 512:(tc + 1) * 512], pbank[tc][:, :], AF.Silu), deps=[last])
                bank_free[tc] = t
                ev.append(t)
            vfree = []
            h, u = o // 4, o % 4
            for b in range(4):
                vfree.append(P.emit("pool", lambda q, h=h, u=u, b=b: q.dma_start(
                    out=vT[b, h, u * 128:(u + 1) * 128, :],
                    in_=vslab[:, b * NPC:(b + 1) * NPC]), deps=ev, dma=True))

        wbb, wbt = cast_weight(wb, 4, "wb")
        bslab = sb("bslab", [4, TOK], F32)
        bev = []
        bfree = None
        for tc in range(4):
            last = None
            for e in range(8):
                dd = []
                if e == 0:
                    dd = wbt + tok_done + ([bfree] if bfree else [])
                last = P.emit("pe", lambda q, e=e, tc=tc: q.matmul(
                    pbeta[:, :], wbb[e][:, :], xnT[e][:, tc * 512:(tc + 1) * 512],
                    start=(e == 0), stop=(e == 7)), deps=dd)
            bfree = P.emit("act", lambda q, tc=tc: q.activation(
                bslab[:, tc * 512:(tc + 1) * 512], pbeta[:, :], AF.Sigmoid), deps=[last])
            bev.append(bfree)
        for b in range(4):
            P.emit("pool", lambda q, b=b: q.dma_start(
                out=bet[b, :, :], in_=bslab[:, b * NPC:(b + 1) * NPC]), deps=bev, dma=True)

        P.barrier()
        P.finalize()
    return nc



def build_launch2(nc, npairs=2, nch=None, debug=False):
    DK, DV, C = 256, 512, 128
    NCH = nch if nch is not None else T // C
    dbg = None
    if debug:
        dbg = nc.dram_tensor("dbg", [8, 128, 512], F32, kind="ExternalOutput").ap()
    qTp = nc.dram_tensor("qTp", [2, DK, T], F32, kind="ExternalInput").ap()
    kTp = nc.dram_tensor("kTp", [2, DK, T], F32, kind="ExternalInput").ap()
    vTp = nc.dram_tensor("vTp", [2, DV, T], F32, kind="ExternalInput").ap()
    betp = nc.dram_tensor("betp", [2, T], F32, kind="ExternalInput").ap()
    ident = nc.dram_tensor("ident", [128, 128], F32, kind="ExternalInput").ap()
    mtu = nc.dram_tensor("mtu", [128, 128], F32, kind="ExternalInput").ap()
    mtl = nc.dram_tensor("mtl", [128, 128], F32, kind="ExternalInput").ap()
    mtui = nc.dram_tensor("mtui", [128, 128], F32, kind="ExternalInput").ap()
    o_out = nc.dram_tensor("o_out", [2, T, DV], F32, kind="ExternalOutput").ap()

    with ExitStack() as stk:
        P = Prog(nc, stk)
        sb = lambda n, s, d: stk.enter_context(nc.sbuf_tensor(n, s, d))

        idf = sb("idf", [128, 128], F32)
        idb = sb("idb", [128, 128], BF16)
        mTU = sb("mTU", [128, 128], F32)
        mTL = sb("mTL", [128, 128], F32)
        mTUI = sb("mTUI", [128, 128], F32)
        c0 = P.emit("sp", lambda q: q.dma_start(out=idf[:, :], in_=ident[:, :]), dma=True)
        c1 = P.emit("pool", lambda q: q.tensor_copy(out=idb[:, :], in_=idf[:, :]), deps=[c0])
        c2 = P.emit("sp", lambda q: q.dma_start(out=mTU[:, :], in_=mtu[:, :]), dma=True)
        c3 = P.emit("sp", lambda q: q.dma_start(out=mTL[:, :], in_=mtl[:, :]), dma=True)
        c4 = P.emit("sp", lambda q: q.dma_start(out=mTUI[:, :], in_=mtui[:, :]), dma=True)
        consts = [c1, c2, c3, c4]

        stage = sb("stage", [128, 512], F32)
        vstage = sb("vstage", [128, 512], F32)
        QT = [sb(f"QT{i}", [128, 128], BF16) for i in range(2)]
        KT = [sb(f"KT{i}", [128, 128], BF16) for i in range(2)]
        Knat = sb("Knat", [128, 256], BF16)
        Vn = sb("Vn", [128, 512], F32)
        bcol = sb("bcol", [128, 1], F32)
        bbc = sb("bbc", [128, 128], F32)
        Wtmp = sb("Wtmp", [128, 128], F32)
        Wf = sb("Wf", [128, 128], F32)
        Wb = [sb(f"Wb{i}", [128, 128], F32) for i in range(2)]
        WTb = [sb(f"WTb{i}", [128, 128], F32) for i in range(2)]
        Pf = sb("Pf", [128, 128], F32)
        Pb = sb("Pb", [128, 128], F32)
        GmTb = sb("GmTb", [128, 128], BF16)
        Rf = sb("Rf", [128, 512], F32)
        Rb = sb("Rb", [128, 512], F32)
        Ub = sb("Ub", [128, 512], BF16)
        osb = sb("osb", [128, 512], F32)
        S0f = sb("S0f", [128, 512], F32)
        S1f = sb("S1f", [128, 512], F32)
        S0b = sb("S0b", [128, 512], BF16)
        S1b = sb("S1b", [128, 512], BF16)

        ps = lambda n, s, d=F32: stk.enter_context(nc.psum_tensor(n, s, d))
        pKK = ps("pKK", [128, 128])
        pG = ps("pG", [128, 128])
        pWT = ps("pWT", [128, 128])
        pVP = ps("pVP", [128, 512])
        pKt = ps("pKt", [128, 256], BF16)
        pKO = ps("pKO", [128, 512])
        pUS = ps("pUS", [128, 512])
        pS1 = ps("pS1", [128, 512])

        st = {
            "stage_free": [], "vstage_free": [], "qkt_free": [], "knat_free": [],
            "v_free": [], "S_used": [], "Sb_used": [], "o_free": [],
            "w_free": [], "kk_free": [], "g_free": [], "wt_free": [], "p_free": [],
            "gm_used": [], "r_free": [], "u_free": [], "bcol_free": [], "bbc_free": [],
            "wf_used": [], "p_used": [], "pb_used": [], "sF_free": [], "sG_free": [],
        }

        for p in range(npairs):
            z0 = P.emit("dve", lambda q: q.memset(S0f[:, :], 0.0), deps=st["S_used"])
            z1 = P.emit("dve", lambda q: q.memset(S1f[:, :], 0.0), deps=st["S_used"])
            z2 = P.emit("pool", lambda q: q.tensor_copy(out=S0b[:, :], in_=S0f[:, :]),
                        deps=[z0] + st["Sb_used"])
            z3 = P.emit("pool", lambda q: q.tensor_copy(out=S1b[:, :], in_=S1f[:, :]),
                        deps=[z1] + st["Sb_used"])
            sb_tok = [z2, z3]
            sf_tok = [z0, z1]
            for ch in range(NCH):
                t0 = ch * C
                lds = []
                for i, (src, dst_off) in enumerate([(qTp, 0), (kTp, 256)]):
                    for j in range(2):
                        lds.append(P.emit("sp", lambda q, src=src, j=j, o=dst_off, p=p, t0=t0: q.dma_start(
                            out=stage[:, (o + j * 128):(o + j * 128 + 128)],
                            in_=src[p, j * 128:(j + 1) * 128, t0:t0 + C]),
                            deps=st["stage_free"], dma=True))
                casts = []
                for j in range(2):
                    casts.append(P.emit("pool", lambda q, j=j: q.tensor_copy(
                        out=QT[j][:, :], in_=stage[:, j * 128:(j + 1) * 128]),
                        deps=[lds[j]] + st["qkt_free"]))
                    casts.append(P.emit("pool", lambda q, j=j: q.tensor_copy(
                        out=KT[j][:, :], in_=stage[:, 256 + j * 128:256 + (j + 1) * 128]),
                        deps=[lds[2 + j]] + st["qkt_free"]))
                st["stage_free"] = casts
                vld = []
                for j in range(4):
                    vld.append(P.emit("sp", lambda q, j=j, p=p, t0=t0: q.dma_start(
                        out=vstage[:, j * 128:(j + 1) * 128],
                        in_=vTp[p, j * 128:(j + 1) * 128, t0:t0 + C]),
                        deps=st["vstage_free"], dma=True))
                bl = P.emit("sp", lambda q, p=p, t0=t0: q.dma_start(
                    out=bcol[:, :], in_=betp[p, t0:t0 + C].unsqueeze(1)),
                    deps=st["bcol_free"], dma=True)
                bb = P.emit("sp", lambda q, p=p, t0=t0: q.dma_start(
                    out=bbc[:, :], in_=betp[p, t0:t0 + C].partition_broadcast(128)),
                    deps=st["bbc_free"], dma=True)

                ktr = []
                tprev = []
                for j in range(2):
                    tt = P.emit("pe", lambda q, j=j: q.transpose(
                        pKt[:, j * 128:(j + 1) * 128], KT[j][:, :], idb[:, :]),
                        deps=[casts[2 * j + 1], c1] + st["knat_free"] + tprev)
                    te = P.emit("dve", lambda q, j=j: q.tensor_copy(
                        out=Knat[:, j * 128:(j + 1) * 128], in_=pKt[:, j * 128:(j + 1) * 128]),
                        deps=[tt])
                    tprev = [te]
                    ktr.append(te)
                vtr = []
                tprev = []
                for j in range(4):
                    tt = P.emit("pe", lambda q, j=j: q.transpose(
                        pVP[:, j * 128:(j + 1) * 128], vstage[:, j * 128:(j + 1) * 128], idf[:, :]),
                        deps=[vld[j], c0] + st["p_free"] + st["v_free"] + tprev)
                    te = P.emit("act", lambda q, j=j: q.copy(
                        Vn[:, j * 128:(j + 1) * 128], pVP[:, j * 128:(j + 1) * 128]),
                        deps=[tt])
                    tprev = [te]
                    vtr.append(te)
                st["vstage_free"] = vtr

                P.emit("pe", lambda q: q.matmul(pKK[:, :], KT[0][:, :], KT[0][:, :],
                                                start=True, stop=False),
                       deps=casts + st["kk_free"])
                mkk = P.emit("pe", lambda q: q.matmul(pKK[:, :], KT[1][:, :], KT[1][:, :],
                                                      start=False, stop=True))
                P.emit("pe", lambda q: q.matmul(pG[:, :], KT[0][:, :], QT[0][:, :],
                                                start=True, stop=False),
                       deps=st["g_free"])
                mg = P.emit("pe", lambda q: q.matmul(pG[:, :], KT[1][:, :], QT[1][:, :],
                                                     start=False, stop=True))

                w1 = P.emit("dve", lambda q: q.scalar_tensor_tensor(
                    Wtmp[:, :], pKK[:, :], -1.0, bbc[:, :], ALU.mult, ALU.mult),
                    deps=[mkk, bb] + consts)
                w2 = P.emit("dve", lambda q: q.tensor_mul(Wf[:, :], Wtmp[:, :], mTU[:, :]),
                    deps=[w1] + st["wf_used"])
                w3 = P.emit("pool", lambda q: q.tensor_copy(out=Wb[0][:, :], in_=Wf[:, :]),
                    deps=[w2] + st["w_free"])
                w4 = P.emit("dve", lambda q: q.tensor_scalar(
                    Wtmp[:, :], pKK[:, :], bcol[:, :], -1.0, ALU.mult, ALU.mult),
                    deps=[w2, bl, w3])
                w5 = P.emit("dve", lambda q: q.tensor_mul(Wtmp[:, :], Wtmp[:, :], mTL[:, :]),
                    deps=[w4])
                w6 = P.emit("pool", lambda q: q.tensor_copy(out=WTb[0][:, :], in_=Wtmp[:, :]),
                    deps=[w5] + st["w_free"])
                st["wf_used"] = [w3]
                st["kk_free"] = [w4]
                gm = P.emit("dve", lambda q: q.tensor_mul(GmTb[:, :], pG[:, :], mTUI[:, :]),
                    deps=[mg] + st["gm_used"])
                p0 = P.emit("dve", lambda q: q.tensor_add(Pf[:, :], idf[:, :], Wf[:, :]),
                    deps=[w2] + st["p_used"])
                p1 = P.emit("pool", lambda q: q.tensor_copy(out=Pb[:, :], in_=Pf[:, :]),
                    deps=[p0] + st["pb_used"])

                cur = 0
                wsq_free = [w3, w6]
                wt_free = st["wt_free"]
                g_free = [gm]
                pupd = [p1]
                for k in range(1, 7):
                    mwt = P.emit("pe", lambda q, cur=cur: q.matmul(
                        pWT[:, :], Wb[cur][:, :], WTb[cur][:, :], start=True, stop=True),
                        deps=wsq_free + wt_free)
                    ewt = P.emit("dve", lambda q, cur=cur: q.tensor_copy(
                        out=WTb[1 - cur][:, :], in_=pWT[:, :]), deps=[mwt])
                    wt_free = [ewt]
                    if k < 6:
                        mw = P.emit("pe", lambda q, cur=cur: q.matmul(
                            pG[:, :], WTb[cur][:, :], Wb[cur][:, :], start=True, stop=True),
                            deps=g_free)
                        ew = P.emit("dve", lambda q, cur=cur: q.tensor_copy(
                            out=Wb[1 - cur][:, :], in_=pG[:, :]), deps=[mw])
                        g_free = [ew]
                    else:
                        ew = ewt
                    mp = P.emit("pe", lambda q, cur=cur: q.matmul(
                        pVP[:, 0:128], WTb[1 - cur][:, :], Pb[:, :], start=True, stop=True),
                        deps=[ewt] + pupd + vtr)
                    pa = P.emit("dve", lambda q: q.tensor_add(Pf[:, :], Pf[:, :], pVP[:, 0:128]),
                        deps=[mp])
                    pc = P.emit("pool", lambda q: q.tensor_copy(out=Pb[:, :], in_=Pf[:, :]),
                        deps=[pa])
                    pupd = [pc]
                    wsq_free = [ewt, ew]
                    cur = 1 - cur
                st["p_free"] = [pa]
                st["p_used"] = pupd
                st["w_free"] = wsq_free
                st["wt_free"] = wt_free
                st["g_free"] = g_free
                st["gm_used"] = g_free

                P.emit("pe", lambda q: q.matmul(pKO[:, :], KT[0][:, :], S0b[:, :],
                                                start=True, stop=False),
                       deps=sb_tok + st["r_free"])
                mks = P.emit("pe", lambda q: q.matmul(pKO[:, :], KT[1][:, :], S1b[:, :],
                                                      start=False, stop=True))
                r1 = P.emit("dve", lambda q: q.scalar_tensor_tensor(
                    Rf[:, :], pKO[:, :], -1.0, Vn[:, :], ALU.mult, ALU.add),
                    deps=[mks] + vtr)
                r2 = P.emit("dve", lambda q: q.tensor_scalar_mul(Rb[:, :], Rf[:, :], bcol[:, :]),
                    deps=[r1])
                st["bcol_free"] = [r2]
                st["bbc_free"] = [w2]
                st["v_free"] = [r1]

                mu_ = P.emit("pe", lambda q: q.matmul(pUS[:, :], Pb[:, :], Rb[:, :],
                                                      start=True, stop=True),
                             deps=[r2] + pupd + st["u_free"] + st["sF_free"])
                ue = P.emit("dve", lambda q: q.tensor_copy(out=Ub[:, :], in_=pUS[:, :]),
                            deps=[mu_])
                st["pb_used"] = [mu_]

                P.emit("pe", lambda q: q.matmul(pKO[:, :], QT[0][:, :], S0b[:, :],
                                                start=True, stop=False),
                       deps=[ue] + st["o_free"] + sb_tok)
                P.emit("pe", lambda q: q.matmul(pKO[:, :], QT[1][:, :], S1b[:, :],
                                                start=False, stop=False))
                mo = P.emit("pe", lambda q: q.matmul(pKO[:, :], GmTb[:, :], Ub[:, :],
                                                     start=False, stop=True), deps=[gm])
                oe = P.emit("act", lambda q: q.copy(osb[:, :], pKO[:, :]),
                            deps=[mo] + st["o_free"])
                ow = P.emit("pool", lambda q, p=p, t0=t0: q.dma_start(
                    out=o_out[p, t0:t0 + C, :], in_=osb[:, :]), deps=[oe], dma=True)
                st["o_free"] = [ow]
                st["r_free"] = [oe]
                st["u_free"] = [mo]

                if debug and p == 0 and ch == 0:
                    for slot, (src, n) in enumerate([(Wb[0], 128), (WTb[0], 128), (Pb, 128),
                                                     (GmTb, 128), (Rb, 512), (Ub, 512)]):
                        dtmp = sb(f"dbg{slot}", [128, n], F32)
                        dt = P.emit("dve", lambda q, src=src, dtmp=dtmp, n=n: q.tensor_copy(
                            out=dtmp[:, :], in_=src[:, 0:n]), deps=[oe, ue, gm, w3, w6] + pupd)
                        P.emit("pool", lambda q, slot=slot, dtmp=dtmp, n=n: q.dma_start(
                            out=dbg[slot, :, 0:n], in_=dtmp[:, :]), deps=[dt], dma=True)

                ms0 = P.emit("pe", lambda q: q.matmul(pUS[:, :], Knat[:, 0:128], Ub[:, :],
                                                      start=True, stop=True),
                             deps=[ue] + ktr + sf_tok)
                ms1 = P.emit("pe", lambda q: q.matmul(pS1[:, :], Knat[:, 128:256], Ub[:, :],
                                                      start=True, stop=True),
                             deps=st["sG_free"])
                sa0 = P.emit("dve", lambda q: q.tensor_add(S0f[:, :], S0f[:, :], pUS[:, :]),
                             deps=[ms0])
                sa1 = P.emit("dve", lambda q: q.tensor_add(S1f[:, :], S1f[:, :], pS1[:, :]),
                             deps=[ms1])
                sc0 = P.emit("pool", lambda q: q.tensor_copy(out=S0b[:, :], in_=S0f[:, :]),
                             deps=[sa0, mo, mks])
                sc1 = P.emit("pool", lambda q: q.tensor_copy(out=S1b[:, :], in_=S1f[:, :]),
                             deps=[sa1, mo, mks])
                sb_tok = [sc0, sc1]
                sf_tok = [sa0, sa1]
                st["sF_free"] = [sa0]
                st["sG_free"] = [sa1]
                st["qkt_free"] = [mo, ms0, ms1, mks]
                st["knat_free"] = [ms0, ms1]
            st["S_used"] = sb_tok + sf_tok
            st["Sb_used"] = sb_tok

        P.barrier()
        P.finalize()
    return nc


def run_launch2(qT, kT, vT, bet):
    DK, DV = 256, 512
    nc = build_launch2(new_nc())
    ident = _ident128()
    mtu = np.triu(np.ones((128, 128), np.float32), 1)
    mtl = np.tril(np.ones((128, 128), np.float32), -1)
    mtui = np.triu(np.ones((128, 128), np.float32), 0)
    in_maps = []
    for c in range(NCORES):
        prs = [(2 * c) // 4, (2 * c) % 4], [(2 * c + 1) // 4, (2 * c + 1) % 4]
        qp = np.stack([qT[b, h] for b, h in prs])
        kp = np.stack([kT[b, h] for b, h in prs])
        vp = np.stack([vT[b, h] for b, h in prs])
        bp = np.stack([bet[b, h] for b, h in prs])
        in_maps.append({
            "qTp": np.ascontiguousarray(qp), "kTp": np.ascontiguousarray(kp),
            "vTp": np.ascontiguousarray(vp), "betp": np.ascontiguousarray(bp),
            "ident": ident, "mtu": mtu, "mtl": mtl, "mtui": mtui,
        })
    res = run_bass_kernel_spmd(nc, in_maps, core_ids=list(range(NCORES)))
    H = 4
    o = np.zeros((B, H, T, DV), np.float32)
    for c in range(NCORES):
        r = res.results[c]
        for i in range(2):
            pair = 2 * c + i
            o[pair // 4, pair % 4] = r["o_out"][i]
    return o



def build_launch3(nc):
    FIN = 2048
    osh = nc.dram_tensor("osh", [TOK, FIN], F32, kind="ExternalInput").ap()
    xsh = nc.dram_tensor("xsh", [TOK, D], F32, kind="ExternalInput").ap()
    wo = nc.dram_tensor("wo", [FIN, D], F32, kind="ExternalInput").ap()
    onw = nc.dram_tensor("onw", [512], F32, kind="ExternalInput").ap()
    ident = nc.dram_tensor("ident", [128, 128], F32, kind="ExternalInput").ap()
    x1sh = nc.dram_tensor("x1sh", [TOK, D], F32, kind="ExternalOutput").ap()

    NT = TOK // 128
    with ExitStack() as stk:
        P = Prog(nc, stk)
        sb = lambda n, s, d: stk.enter_context(nc.sbuf_tensor(n, s, d))
        ps = lambda n, s, d=F32: stk.enter_context(nc.psum_tensor(n, s, d))

        idf = sb("idf", [128, 128], F32)
        idb = sb("idb", [128, 128], BF16)
        t_id = P.emit("sp", lambda q: q.dma_start(out=idf[:, :], in_=ident[:, :]), dma=True)
        t_idb = P.emit("pool", lambda q: q.tensor_copy(out=idb[:, :], in_=idf[:, :]), deps=[t_id])
        onwbc = sb("onwbc", [128, 512], F32)
        t_onw = P.emit("sp", lambda q: q.dma_start(out=onwbc[:, :], in_=onw.partition_broadcast(128)), dma=True)

        wstage = sb("wstage", [128, 1024], F32)
        wob = [sb(f"wob{j}", [128, 1024], BF16) for j in range(16)]
        wtoks = []
        sfree = []
        for j in range(16):
            tl = P.emit("sp", lambda q, j=j: q.dma_start(
                out=wstage[:, :], in_=wo[j * 128:(j + 1) * 128, :]), deps=sfree, dma=True)
            tc = P.emit("pool", lambda q, j=j: q.tensor_copy(out=wob[j][:, :], in_=wstage[:, :]),
                        deps=[tl])
            sfree = [tc]
            wtoks.append(tc)

        ot = sb("ot", [128, FIN], F32)
        xt = sb("xt", [128, D], F32)
        sq = sb("sq", [128, 512], F32)
        onb = sb("onb", [128, FIN], BF16)
        onT = sb("onT", [128, FIN], BF16)
        col = sb("col", [128, 1], F32)
        col2 = sb("col2", [128, 1], F32)
        yt = sb("yt", [128, D], F32)
        ptb = ps("ptb", [128, 128], BF16)
        pk = [ps(f"pk{u}", [128, 512]) for u in range(2)]

        ld_free, on_free, onT_free, y_free, pk_free, sq_free = [], [], [], [None, None], [], []
        for i in range(NT):
            tl = P.emit("sp", lambda q, i=i: q.dma_start(out=ot[:, :], in_=osh[i * 128:(i + 1) * 128, :]),
                        deps=ld_free, dma=True)
            tx = P.emit("sp", lambda q, i=i: q.dma_start(out=xt[:, :], in_=xsh[i * 128:(i + 1) * 128, :]),
                        deps=y_free[1:] if y_free[0] is None else y_free, dma=True)
            hev = []
            for h in range(4):
                s1 = P.emit("act", lambda q, h=h: q.square(sq[:, :], ot[:, h * 512:(h + 1) * 512]),
                            deps=[tl] + hev[-1:] + (sq_free if h == 0 else []))
                s2 = P.emit("dve", lambda q: q.reduce_sum(col[:, :], sq[:, :], axis=AX.X), deps=[s1])
                s3 = P.emit("dve", lambda q: q.tensor_scalar(col2[:, :], col[:, :], 1.0 / 512, 1e-5,
                                                             ALU.mult, ALU.add), deps=[s2])
                s4 = P.emit("act", lambda q: q.sqrt(col[:, :], col2[:, :]), deps=[s3])
                s5 = P.emit("dve", lambda q: q.reciprocal(col2[:, :], col[:, :]), deps=[s4])
                s6 = P.emit("dve", lambda q, h=h: q.scalar_tensor_tensor(
                    onb[:, h * 512:(h + 1) * 512], ot[:, h * 512:(h + 1) * 512],
                    col2[:, :], onwbc[:, :], ALU.mult, ALU.mult),
                    deps=[s5, t_onw] + on_free)
                hev.append(s6)
            ttoks = []
            tprev = []
            for j in range(16):
                tt = P.emit("pe", lambda q, j=j: q.transpose(
                    ptb[:, :], onb[:, j * 128:(j + 1) * 128], idb[:, :]),
                    deps=hev + [t_idb] + tprev + onT_free)
                te = P.emit("dve", lambda q, j=j: q.tensor_copy(
                    out=onT[:, j * 128:(j + 1) * 128], in_=ptb[:, :]), deps=[tt])
                tprev = [te]
                ttoks.append(te)
            on_free = [ttoks[-1]]
            ld_free = hev
            mm = [None, None]
            for u in range(2):
                for j in range(16):
                    dd = []
                    if j == 0:
                        dd = ttoks + wtoks + ([pk_free[u]] if len(pk_free) > u and pk_free[u] else [])
                    mm[u] = P.emit("pe", lambda q, u=u, j=j: q.matmul(
                        pk[u][:, :], onT[:, j * 128:(j + 1) * 128],
                        wob[j][:, u * 512:(u + 1) * 512],
                        start=(j == 0), stop=(j == 15)), deps=dd)
            onT_free = [mm[0], mm[1]]
            e1 = P.emit("dve", lambda q: q.scalar_tensor_tensor(
                yt[:, 0:512], xt[:, 0:512], 2.0, pk[0][:, :], ALU.mult, ALU.add),
                deps=[mm[0], tx])
            e2 = P.emit("dve", lambda q: q.scalar_tensor_tensor(
                yt[:, 512:1024], xt[:, 512:1024], 2.0, pk[1][:, :], ALU.mult, ALU.add),
                deps=[mm[1], tx])
            pk_free = [e1, e2]
            l1 = P.emit("dve", lambda q: q.reduce_sum(col[:, :], yt[:, :], axis=AX.X), deps=[e1, e2])
            l2_ = P.emit("dve", lambda q: q.tensor_scalar_mul(col2[:, :], col[:, :], 1.0 / D), deps=[l1])
            l3 = P.emit("dve", lambda q: q.tensor_scalar_sub(yt[:, :], yt[:, :], col2[:, :]), deps=[l2_])
            l4 = P.emit("act", lambda q: q.square(sq[:, :], yt[:, 0:512]), deps=[l3])
            l4b = P.emit("dve", lambda q: q.reduce_sum(col[:, :], sq[:, :], axis=AX.X), deps=[l4])
            l4c = P.emit("act", lambda q: q.square(sq[:, :], yt[:, 512:1024]), deps=[l4b])
            l4d = P.emit("dve", lambda q: q.reduce_sum(col2[:, :], sq[:, :], axis=AX.X), deps=[l4c])
            l5 = P.emit("dve", lambda q: q.tensor_add(col[:, :], col[:, :], col2[:, :]), deps=[l4d])
            l6 = P.emit("dve", lambda q: q.tensor_scalar(col2[:, :], col[:, :], 1.0 / D, 1e-5,
                                                         ALU.mult, ALU.add), deps=[l5])
            l7 = P.emit("act", lambda q: q.sqrt(col[:, :], col2[:, :]), deps=[l6])
            l8 = P.emit("dve", lambda q: q.reciprocal(col2[:, :], col[:, :]), deps=[l7])
            l9 = P.emit("dve", lambda q: q.tensor_scalar_mul(yt[:, :], yt[:, :], col2[:, :]), deps=[l8])
            tw = P.emit("pool", lambda q, i=i: q.dma_start(
                out=x1sh[i * 128:(i + 1) * 128, :], in_=yt[:, :]), deps=[l9], dma=True)
            y_free = [tw, l9]
            sq_free = [l4d]

        P.barrier()
        P.finalize()
    return nc


def run_launch3(inputs, o):
    on = np.ascontiguousarray(np.transpose(o, (0, 2, 1, 3)).reshape(B, T, 2048))
    x = inputs["L0_attn_x"]
    nc = build_launch3(new_nc())
    ident = _ident128()
    in_maps = []
    for c in range(NCORES):
        in_maps.append({
            "osh": np.ascontiguousarray(on[:, c * NPC:(c + 1) * NPC, :].reshape(TOK, 2048)),
            "xsh": np.ascontiguousarray(x[:, c * NPC:(c + 1) * NPC, :].reshape(TOK, D)),
            "wo": inputs["L0_attn_Wo"], "onw": inputs["L0_attn_o_norm_w"],
            "ident": ident,
        })
    res = run_bass_kernel_spmd(nc, in_maps, core_ids=list(range(NCORES)))
    x1 = np.zeros((B, T, D), np.float32)
    for c in range(NCORES):
        x1[:, c * NPC:(c + 1) * NPC, :] = res.results[c]["x1sh"].reshape(B, NPC, D)
    return x1



def build_launch4(nc, nexp=32, upto=5, sub=5):
    NE = 32
    x1h = nc.dram_tensor("x1sh", [TOK, D], F32, kind="ExternalInput").ap()
    gwT = nc.dram_tensor("gwT", [D, NE], F32, kind="ExternalInput").ap()
    guwT = nc.dram_tensor("guwT", [NE, D, 1024], F32, kind="ExternalInput").ap()
    dwT = nc.dram_tensor("dwT", [NE, 512, 1024], F32, kind="ExternalInput").ap()
    ident = nc.dram_tensor("ident", [128, 128], F32, kind="ExternalInput").ap()
    combh = nc.dram_tensor("combh", [NE, TOK], F32, kind="ExternalOutput").ap()
    x2sh = nc.dram_tensor("x2sh", [TOK, D], F32, kind="ExternalOutput").ap()

    NT = TOK // 128
    with ExitStack() as stk:
        P = Prog(nc, stk)
        sb = lambda n, s, d: stk.enter_context(nc.sbuf_tensor(n, s, d))
        ps = lambda n, s, d=F32: stk.enter_context(nc.psum_tensor(n, s, d))

        idf = sb("idf", [128, 128], F32)
        t_id = P.emit("sp", lambda q: q.dma_start(out=idf[:, :], in_=ident[:, :]), dma=True)

        x1Tb = [sb(f"x1Tb{j}", [128, TOK], BF16) for j in range(8)]
        gwTf = [sb(f"gwTf{j}", [128, NE], F32) for j in range(8)]
        xt = sb("xt", [128, D], F32)
        xtf = sb("xtf", [128, 128], F32)
        ltmp = sb("ltmp", [NE, 128], F32)
        pT1 = ps("pT1", [128, 128])
        pR = ps("pR", [NE, 512])

        gw_toks = []
        for j in range(8):
            tl = P.emit("sp", lambda q, j=j: q.dma_start(
                out=gwTf[j][:, :], in_=gwT[j * 128:(j + 1) * 128, :]), dma=True)
            gw_toks.append(tl)

        gnat = sb("gnat", [128, NT * NE], F32)
        comb = sb("comb", [128, NT * NE], F32)
        work = sb("work", [128, NE], F32)
        tmp = sb("tmp", [128, NE], F32)
        col = sb("col", [128, 1], F32)
        col2 = sb("col2", [128, 1], F32)
        tsum = sb("tsum", [128, 1], F32)
        combT = sb("combT", [NE, TOK], F32)

        xa_done = []
        gn_toks = []
        prev_e = []
        tprev = []
        mm_prev = []
        ev_prev = []
        tg_prev = []
        for i in range(NT):
            tl = P.emit("sp", lambda q, i=i: q.dma_start(
                out=xt[:, :], in_=x1h[i * 128:(i + 1) * 128, :]), deps=prev_e, dma=True)
            prev_e = []
            mm = None
            for j in range(8):
                tt = P.emit("pe", lambda q, j=j: q.transpose(
                    pT1[:, :], xt[:, j * 128:(j + 1) * 128], idf[:, :]),
                    deps=[tl, t_id] + tprev + tg_prev)
                te = P.emit("dve", lambda q, i=i, j=j: q.tensor_copy(
                    out=x1Tb[j][:, i * 128:(i + 1) * 128], in_=pT1[:, :]), deps=[tt])
                if sub >= 2:
                    tf = P.emit("act", lambda q: q.copy(xtf[:, :], pT1[:, :]),
                                deps=[tt, te] + mm_prev)
                    tprev = [te, tf]
                else:
                    tf = te
                    tprev = [te]
                if sub >= 3:
                    mm = P.emit("pe", lambda q, j=j: q.matmul(
                        pR[:, 0:128], gwTf[j][:, :], xtf[:, :],
                        start=(j == 0), stop=(j == 7)),
                        deps=[tf] + gw_toks + (ev_prev if j == 0 else []))
                    mm_prev = [mm]
                prev_e = [te]
            if sub >= 4:
                ev = P.emit("dve", lambda q: q.tensor_copy(out=ltmp[:, :], in_=pR[0:NE, 0:128]),
                            deps=[mm])
                ev_prev = [ev]
            if sub >= 5:
                tt2 = P.emit("pe", lambda q: q.transpose(
                    pT1[0:128, 0:NE], ltmp[:, :], idf[0:NE, 0:NE]), deps=[ev] + tprev)
                tg = P.emit("dve", lambda q, i=i: q.tensor_copy(
                    out=gnat[:, i * NE:(i + 1) * NE], in_=pT1[0:128, 0:NE]), deps=[tt2])
                tg_prev = [tg]
                gn_toks.append(tg)
            xa_done.append(te)

        rt_toks = []
        if upto < 2:
            P.barrier(); P.finalize(); return nc
        for i in range(NT):
            s1 = P.emit("dve", lambda q, i=i: q.reduce_max(col[:, :], gnat[:, i * NE:(i + 1) * NE],
                                                           axis=AX.X), deps=[gn_toks[i]] + rt_toks[-1:])
            s2 = P.emit("dve", lambda q: q.tensor_scalar_mul(col2[:, :], col[:, :], -1.0), deps=[s1])
            s3 = P.emit("act", lambda q, i=i: q.activation(
                gnat[:, i * NE:(i + 1) * NE], gnat[:, i * NE:(i + 1) * NE], AF.Exp,
                bias=col2[:, :], scale=1.0), deps=[s2])
            s4 = P.emit("dve", lambda q, i=i: q.reduce_sum(col[:, :], gnat[:, i * NE:(i + 1) * NE],
                                                           axis=AX.X), deps=[s3])
            s5 = P.emit("dve", lambda q: q.reciprocal(col2[:, :], col[:, :]), deps=[s4])
            s6 = P.emit("dve", lambda q, i=i: q.tensor_scalar_mul(
                gnat[:, i * NE:(i + 1) * NE], gnat[:, i * NE:(i + 1) * NE], col2[:, :]), deps=[s5])
            s7 = P.emit("dve", lambda q, i=i: q.tensor_copy(out=work[:, :], in_=gnat[:, i * NE:(i + 1) * NE]),
                        deps=[s6])
            tprev_it = [s7]
            for it in range(8):
                m1 = P.emit("dve", lambda q: q.reduce_max(col[:, :], work[:, :], axis=AX.X),
                            deps=tprev_it)
                if it == 0:
                    m2 = P.emit("dve", lambda q: q.tensor_copy(out=tsum[:, :], in_=col[:, :]), deps=[m1])
                else:
                    m2 = P.emit("dve", lambda q: q.tensor_add(tsum[:, :], tsum[:, :], col[:, :]), deps=[m1])
                m3 = P.emit("dve", lambda q: q.tensor_single_scalar(tmp[:, :], work[:, :], col[:, :],
                                                                    ALU.is_ge), deps=[m1])
                m4 = P.emit("dve", lambda q: q.scalar_tensor_tensor(
                    work[:, :], tmp[:, :], -1e9, work[:, :], ALU.mult, ALU.add), deps=[m3, m2])
                tprev_it = [m4]
            m5 = P.emit("dve", lambda q: q.tensor_single_scalar(tmp[:, :], work[:, :], -1e8,
                                                                ALU.is_lt), deps=tprev_it)
            m6 = P.emit("dve", lambda q: q.reciprocal(col[:, :], tsum[:, :]), deps=[m5])
            m7 = P.emit("dve", lambda q, i=i: q.scalar_tensor_tensor(
                comb[:, i * NE:(i + 1) * NE], gnat[:, i * NE:(i + 1) * NE], col[:, :], tmp[:, :],
                ALU.mult, ALU.mult), deps=[m6])
            rt_toks.append(m7)
        if upto < 3:
            P.barrier(); P.finalize(); return nc
        if upto < 3:
            P.barrier(); P.finalize(); return nc
        cw_toks = []
        tprev3 = []
        for i in range(NT):
            tt = P.emit("pe", lambda q, i=i: q.transpose(
                pT1[0:NE, 0:128], comb[:, i * NE:(i + 1) * NE], idf[:, :]),
                deps=[rt_toks[i]] + tprev3)
            te = P.emit("dve", lambda q, i=i: q.tensor_copy(
                out=combT[:, i * 128:(i + 1) * 128], in_=pT1[0:NE, 0:128]), deps=[tt])
            tprev3 = [te]
        cwr = P.emit("pool", lambda q: q.dma_start(out=combh[:, :], in_=combT[:, :]),
                     deps=tprev3, dma=True)

        if upto < 4:
            P.barrier(); P.finalize(); return nc
        guwb = [[sb(f"guwb{u}_{j}", [128, 1024], BF16) for j in range(8)] for u in range(2)]
        dwb = [[sb(f"dwb{u}_{j}", [128, 1024], BF16) for j in range(4)] for u in range(2)]
        wstage = sb("wstage", [128, 1024], F32)
        bcb = sb("bcb", [128, TOK], F32)
        bc = [bcb, bcb]
        hs = [sb(f"hs{j}", [128, TOK], BF16) for j in range(4)]
        sgt = sb("sgt", [128, 512], BF16)
        htmp = sb("htmp", [128, 512], BF16)
        accT = [sb(f"accT{j}", [128, TOK], F32) for j in range(8)]
        pG = ps("pG", [128, 512])
        pU2 = ps("pU2", [128, 512])
        pD = ps("pD", [128, 512])

        acz = [P.emit("dve", lambda q, j=j: q.memset(accT[j][:, :], 0.0)) for j in range(8)]
        wfree = [[], []]
        stg_free = []
        bc_free = [[cwr], [cwr]]
        hev_all = []
        dones = []
        pg_free, pu_free, pd_free, sg_free, ht_free = [], [], [], [], []
        for e in range(nexp):
            u = e % 2
            wt = []
            for j in range(8):
                tl = P.emit("sp", lambda q, e=e, j=j: q.dma_start(
                    out=wstage[:, :], in_=guwT[e, j * 128:(j + 1) * 128, :]),
                    deps=stg_free, dma=True)
                tc = P.emit("pool", lambda q, u=u, j=j: q.tensor_copy(
                    out=guwb[u][j][:, :], in_=wstage[:, :]), deps=[tl] + wfree[u])
                stg_free = [tc]
                wt.append(tc)
            for j in range(4):
                tl = P.emit("sp", lambda q, e=e, j=j: q.dma_start(
                    out=wstage[:, :], in_=dwT[e, j * 128:(j + 1) * 128, :]),
                    deps=stg_free, dma=True)
                tc = P.emit("pool", lambda q, u=u, j=j: q.tensor_copy(
                    out=dwb[u][j][:, :], in_=wstage[:, :]), deps=[tl] + wfree[u])
                stg_free = [tc]
                wt.append(tc)
            tb = P.emit("sp", lambda q, e=e, u=u: q.dma_start(
                out=bc[u][:, :], in_=combh[e, :].partition_broadcast(128)),
                deps=bc_free[0] + bc_free[1], dma=True)
            hev = []
            for jj in range(4):
                for tc_ in range(4):
                    lg = lu = None
                    for dj in range(8):
                        dd = (wt + xa_done + pg_free + hev_all) if dj == 0 else []
                        lg = P.emit("pe", lambda q, u=u, jj=jj, tc_=tc_, dj=dj: q.matmul(
                            pG[:, :], guwb[u][dj][:, jj * 128:(jj + 1) * 128],
                            x1Tb[dj][:, tc_ * 512:(tc_ + 1) * 512],
                            start=(dj == 0), stop=(dj == 7)), deps=dd)
                    for dj in range(8):
                        dd = (pu_free if dj == 0 else [])
                        lu = P.emit("pe", lambda q, u=u, jj=jj, tc_=tc_, dj=dj: q.matmul(
                            pU2[:, :], guwb[u][dj][:, 512 + jj * 128:512 + (jj + 1) * 128],
                            x1Tb[dj][:, tc_ * 512:(tc_ + 1) * 512],
                            start=(dj == 0), stop=(dj == 7)), deps=dd)
                    a1 = P.emit("act", lambda q: q.activation(sgt[:, :], pG[:, :], AF.Silu),
                                deps=[lg] + sg_free)
                    pg_free = [a1]
                    a2 = P.emit("dve", lambda q: q.scalar_tensor_tensor(
                        htmp[:, :], pU2[:, :], 1.0, sgt[:, :], ALU.mult, ALU.mult),
                        deps=[a1, lu] + ht_free)
                    pu_free = [a2]
                    sg_free = [a2]
                    a3 = P.emit("pool", lambda q, jj=jj, tc_=tc_, u=u: q.tensor_mul(
                        hs[jj][:, tc_ * 512:(tc_ + 1) * 512], htmp[:, :],
                        bc[u][:, tc_ * 512:(tc_ + 1) * 512]), deps=[a2, tb])
                    ht_free = [a3]
                    hev.append(a3)
            hev_all = hev
            ddone = []
            for o in range(8):
                for tc_ in range(4):
                    last = None
                    for jj in range(4):
                        dd = (hev + pd_free) if jj == 0 else []
                        last = P.emit("pe", lambda q, u=u, o=o, tc_=tc_, jj=jj: q.matmul(
                            pD[:, :], dwb[u][jj][:, o * 128:(o + 1) * 128],
                            hs[jj][:, tc_ * 512:(tc_ + 1) * 512],
                            start=(jj == 0), stop=(jj == 3)), deps=dd)
                    ad = P.emit("dve", lambda q, o=o, tc_=tc_: q.tensor_add(
                        accT[o][:, tc_ * 512:(tc_ + 1) * 512],
                        accT[o][:, tc_ * 512:(tc_ + 1) * 512], pD[:, :]),
                        deps=[last] + ([acz[o]] if e == 0 else []))
                    pd_free = [ad]
                    ddone.append(ad)
            wfree[u] = [lg, lu, last]
            bc_free[0] = [a3]
            bc_free[1] = [a3]
            dones = ddone

        if upto < 5:
            P.barrier(); P.finalize(); return nc
        yt = sb("yt", [128, D], F32)
        sq2 = sb("sq2", [128, D], F32)
        tprev4 = []
        xfree = []
        yfree = []
        for i in range(NT):
            tx = P.emit("sp", lambda q, i=i: q.dma_start(
                out=xt[:, :], in_=x1h[i * 128:(i + 1) * 128, :]),
                deps=(xa_done + dones) if i == 0 else xfree, dma=True)
            ttoks = []
            for j in range(8):
                tt = P.emit("pe", lambda q, i=i, j=j: q.transpose(
                    pT1[:, :], accT[j][:, i * 128:(i + 1) * 128], idf[:, :]),
                    deps=dones + tprev4)
                te = P.emit("dve", lambda q, j=j: q.tensor_add(
                    yt[:, j * 128:(j + 1) * 128], xt[:, j * 128:(j + 1) * 128], pT1[:, :]),
                    deps=[tt, tx] + yfree)
                tprev4 = [te]
                ttoks.append(te)
            xfree = ttoks
            l1 = P.emit("dve", lambda q: q.reduce_sum(col[:, :], yt[:, :], axis=AX.X), deps=ttoks)
            l2_ = P.emit("dve", lambda q: q.tensor_scalar_mul(col2[:, :], col[:, :], 1.0 / D), deps=[l1])
            l3 = P.emit("dve", lambda q: q.tensor_scalar_sub(yt[:, :], yt[:, :], col2[:, :]), deps=[l2_])
            l4 = P.emit("act", lambda q: q.square(sq2[:, :], yt[:, :]), deps=[l3])
            l5 = P.emit("dve", lambda q: q.reduce_sum(col[:, :], sq2[:, :], axis=AX.X), deps=[l4])
            l6 = P.emit("dve", lambda q: q.tensor_scalar(col2[:, :], col[:, :], 1.0 / D, 1e-5,
                                                         ALU.mult, ALU.add), deps=[l5])
            l7 = P.emit("act", lambda q: q.sqrt(col[:, :], col2[:, :]), deps=[l6])
            l8 = P.emit("dve", lambda q: q.reciprocal(col2[:, :], col[:, :]), deps=[l7])
            l9 = P.emit("dve", lambda q: q.tensor_scalar_mul(yt[:, :], yt[:, :], col2[:, :]), deps=[l8])
            tw = P.emit("pool", lambda q, i=i: q.dma_start(
                out=x2sh[i * 128:(i + 1) * 128, :], in_=yt[:, :]), deps=[l9], dma=True)
            yfree = [tw, l9]

        P.barrier()
        P.finalize()
    return nc


def run_launch4(inputs, x1, nexp=32):
    nc = build_launch4(new_nc(), nexp=nexp)
    ident = _ident128()
    gwT = np.ascontiguousarray(inputs["L0_ffn_gate_w"].T)
    guwT = np.ascontiguousarray(inputs["L0_ffn_gate_up_w"].transpose(0, 2, 1))
    dwT = np.ascontiguousarray(inputs["L0_ffn_down_w"].transpose(0, 2, 1))
    in_maps = []
    for c in range(NCORES):
        in_maps.append({
            "x1sh": np.ascontiguousarray(x1[:, c * NPC:(c + 1) * NPC, :].reshape(TOK, D)),
            "gwT": gwT, "guwT": guwT, "dwT": dwT, "ident": ident,
        })
    res = run_bass_kernel_spmd(nc, in_maps, core_ids=list(range(NCORES)))
    x2 = np.zeros((B, T, D), np.float32)
    for c in range(NCORES):
        x2[:, c * NPC:(c + 1) * NPC, :] = res.results[c]["x2sh"].reshape(B, NPC, D)
    return x2



def build_launch5(nc):
    x2h = nc.dram_tensor("x2sh", [TOK, D], F32, kind="ExternalInput").ap()
    wqT = nc.dram_tensor("wqT", [D, D], F32, kind="ExternalInput").ap()
    wkT = nc.dram_tensor("wkT", [D, D], F32, kind="ExternalInput").ap()
    wvT = nc.dram_tensor("wvT", [D, D], F32, kind="ExternalInput").ap()
    ident = nc.dram_tensor("ident", [128, 128], F32, kind="ExternalInput").ap()
    qTo = nc.dram_tensor("qTo", [B, D, NPC], F32, kind="ExternalOutput").ap()
    kTo = nc.dram_tensor("kTo", [B, D, NPC], F32, kind="ExternalOutput").ap()
    Vo = nc.dram_tensor("Vo", [B, NPC, D], F32, kind="ExternalOutput").ap()

    NT = TOK // 128
    with ExitStack() as stk:
        P = Prog(nc, stk)
        sb = lambda n, s, d=F32: stk.enter_context(nc.sbuf_tensor(n, s, d))
        ps = lambda n, s, d=F32: stk.enter_context(nc.psum_tensor(n, s, d))
        idf = sb("idf", [128, 128])
        t_id = P.emit("sp", lambda q: q.dma_start(out=idf[:, :], in_=ident[:, :]), dma=True)
        xTb = [sb(f"xTb{j}", [128, TOK], BF16) for j in range(8)]
        xt = sb("xt", [128, D])
        pT1 = ps("pT1", [128, 128])
        xa = []
        tprev = []
        prev_e = []
        for i in range(NT):
            tl = P.emit("sp", lambda q, i=i: q.dma_start(
                out=xt[:, :], in_=x2h[i * 128:(i + 1) * 128, :]), deps=prev_e, dma=True)
            prev_e = []
            for j in range(8):
                tt = P.emit("pe", lambda q, j=j: q.transpose(
                    pT1[:, :], xt[:, j * 128:(j + 1) * 128], idf[:, :]),
                    deps=[tl, t_id] + tprev)
                te = P.emit("dve", lambda q, i=i, j=j: q.tensor_copy(
                    out=xTb[j][:, i * 128:(i + 1) * 128], in_=pT1[:, :]), deps=[tt])
                tprev = [te]
                prev_e = [te]
            xa.append(prev_e[0])

        wstage = sb("wstage", [128, 1024])
        wb = [sb(f"wb{j}", [128, 1024], BF16) for j in range(8)]
        pk = [ps(f"pk{u}", [128, 512]) for u in range(2)]
        slab = sb("slab", [128, TOK])

        def cast_w(hbm, deps):
            toks = []
            sfree = list(deps)
            for j in range(8):
                tl = P.emit("sp", lambda q, j=j: q.dma_start(
                    out=wstage[:, :], in_=hbm[j * 128:(j + 1) * 128, :]), deps=sfree, dma=True)
                tc = P.emit("pool", lambda q, j=j: q.tensor_copy(out=wb[j][:, :], in_=wstage[:, :]),
                            deps=[tl] + list(deps))
                sfree = [tc]
                toks.append(tc)
            return toks

        bank_free = [None, None]
        sl_free = []

        def proj_T(out_hbm, scale, wtoks):
            nonlocal bank_free, sl_free
            for o in range(8):
                ev = []
                for tc_ in range(4):
                    u = tc_ % 2
                    last = None
                    for e in range(8):
                        dd = []
                        if e == 0:
                            dd = wtoks + xa + sl_free + ([bank_free[u]] if bank_free[u] else [])
                        last = P.emit("pe", lambda q, e=e, o=o, tc_=tc_: q.matmul(
                            pk[tc_ % 2][:, :], wb[e][:, o * 128:(o + 1) * 128],
                            xTb[e][:, tc_ * 512:(tc_ + 1) * 512],
                            start=(e == 0), stop=(e == 7)), deps=dd)
                    t = P.emit("dve", lambda q, tc_=tc_: q.tensor_scalar_mul(
                        slab[:, tc_ * 512:(tc_ + 1) * 512], pk[tc_ % 2][:, :], scale), deps=[last])
                    bank_free[u] = t
                    ev.append(t)
                sl_free = []
                for b in range(4):
                    tw = P.emit("pool", lambda q, o=o, b=b: q.dma_start(
                        out=out_hbm[b, o * 128:(o + 1) * 128, :],
                        in_=slab[:, b * NPC:(b + 1) * NPC]), deps=ev, dma=True)
                    sl_free.append(tw)

        wt = cast_w(wqT, [t_id])
        proj_T(qTo, 0.125, wt)
        wt = cast_w(wkT, sl_free)
        proj_T(kTo, 1.0, wt)
        wt = cast_w(wvT, sl_free)
        vt = sb("vt", [128, D])
        vfree = []
        for i in range(NT):
            b, r0 = (i * 128) // NPC, (i * 128) % NPC
            ev = []
            for half in range(2):
                u = half
                last = None
                for e in range(8):
                    dd = []
                    if e == 0:
                        dd = wt + xa + vfree + ([bank_free[u]] if bank_free[u] else [])
                    last = P.emit("pe", lambda q, e=e, i=i, half=half: q.matmul(
                        pk[half][:, :], xTb[e][:, i * 128:(i + 1) * 128],
                        wb[e][:, half * 512:(half + 1) * 512],
                        start=(e == 0), stop=(e == 7)), deps=dd)
                t = P.emit("dve", lambda q, half=half: q.tensor_copy(
                    out=vt[:, half * 512:(half + 1) * 512], in_=pk[half][:, :]), deps=[last])
                bank_free[u] = t
                ev.append(t)
            vfree = [P.emit("pool", lambda q, b=b, r0=r0: q.dma_start(
                out=Vo[b, r0:r0 + 128, :], in_=vt[:, :]), deps=ev, dma=True)]

        P.barrier()
        P.finalize()
    return nc


def run_launch5(inputs, x2):
    nc = build_launch5(new_nc())
    ident = _ident128()
    in_maps = []
    for c in range(NCORES):
        in_maps.append({
            "x2sh": np.ascontiguousarray(x2[:, c * NPC:(c + 1) * NPC, :].reshape(TOK, D)),
            "wqT": np.ascontiguousarray(inputs["L1_attn_Wq"].T),
            "wkT": np.ascontiguousarray(inputs["L1_attn_Wk"].T),
            "wvT": np.ascontiguousarray(inputs["L1_attn_Wv"].T),
            "ident": ident,
        })
    res = run_bass_kernel_spmd(nc, in_maps, core_ids=list(range(NCORES)))
    qT = np.zeros((B, D, T), np.float32)
    kT = np.zeros((B, D, T), np.float32)
    V = np.zeros((B, T, D), np.float32)
    for c in range(NCORES):
        r = res.results[c]
        sl = slice(c * NPC, (c + 1) * NPC)
        qT[:, :, sl] = r["qTo"]
        kT[:, :, sl] = r["kTo"]
        V[:, sl, :] = r["Vo"]
    return qT, kT, V



def _toeplitz(brow, h, J0, width=512):
    a = brow[h, J0:J0 - 128:-1].unsqueeze(1)
    v = a.ap.copy()
    v.pop()
    v.append((1, width))
    a.ap = v
    return a


def build_launch6(nc, nheads=16, nslots=16):
    qTs = nc.dram_tensor("qTs", [D, TOK], F32, kind="ExternalInput").ap()
    kTs = nc.dram_tensor("kTs", [D, T], F32, kind="ExternalInput").ap()
    Vs = nc.dram_tensor("Vs", [T, D], F32, kind="ExternalInput").ap()
    btiles = nc.dram_tensor("btiles", [16, 16, 128, 512], F32, kind="ExternalInput").ap()
    tau16 = nc.dram_tensor("tau16", [16], F32, kind="ExternalInput").ap()
    roffs = nc.dram_tensor("roffs", [2], F32, kind="ExternalInput").ap()
    x2sl = nc.dram_tensor("x2sl", [TOK, D], F32, kind="ExternalInput").ap()
    woT = nc.dram_tensor("woT", [D, D], F32, kind="ExternalInput").ap()
    ident = nc.dram_tensor("ident", [128, 128], F32, kind="ExternalInput").ap()
    x3sl = nc.dram_tensor("x3sl", [TOK, D], F32, kind="ExternalOutput").ap()

    with ExitStack() as stk:
        P = Prog(nc, stk)
        sb = lambda n, s, d=F32: stk.enter_context(nc.sbuf_tensor(n, s, d))
        ps = lambda n, s, d=F32: stk.enter_context(nc.psum_tensor(n, s, d))
        idf = sb("idf", [128, 128])
        idb = sb("idb", [128, 128], BF16)
        c0 = P.emit("sp", lambda q: q.dma_start(out=idf[:, :], in_=ident[:, :]), dma=True)
        c1 = P.emit("pool", lambda q: q.tensor_copy(out=idb[:, :], in_=idf[:, :]), deps=[c0])
        roffc = sb("roffc", [128, 2])
        c2 = P.emit("sp", lambda q: q.dma_start(out=roffc[:, :], in_=roffs.partition_broadcast(128)), dma=True)
        onesb = sb("onesb", [128, 1], BF16)
        c3 = P.emit("dve", lambda q: q.memset(onesb[:, :], 1.0))
        iotac = sb("iotac", [128, 1])
        c4 = P.emit("pool", lambda q: q.iota(iotac[:, :], [[1, 1]], base=0, channel_multiplier=1,
                                             allow_small_or_imprecise_dtypes=True))
        iotar = sb("iotar", [1, 128])
        c5 = P.emit("pool", lambda q: q.iota(iotar[:, :], [[1, 128]], base=0, channel_multiplier=0,
                                             allow_small_or_imprecise_dtypes=True))
        wstage = sb("wstage", [128, 1024])
        wotb = [sb(f"wotb{j}", [128, 1024], BF16) for j in range(8)]
        wo_toks = []
        sfree = []
        for j in range(8):
            tl = P.emit("sp", lambda q, j=j: q.dma_start(
                out=wstage[:, :], in_=woT[j * 128:(j + 1) * 128, :]), deps=sfree, dma=True)
            tc = P.emit("pool", lambda q, j=j: q.tensor_copy(out=wotb[j][:, :], in_=wstage[:, :]),
                        deps=[tl])
            sfree = [tc]
            wo_toks.append(tc)

        Oacc = [sb(f"Oacc{s}", [128, D]) for s in range(nslots)]
        oz = [P.emit("dve", lambda q, s=s: q.memset(Oacc[s][:, :], 0.0)) for s in range(nslots)]

        kstage = sb("kstage", [64, T])
        kb = sb("kb", [64, T], BF16)
        vstage = sb("vstage", [128, 2048])
        vb = sb("vb", [128, 2048], BF16)
        btstage = sb("btstage", [128, 512])
        btb = sb("btb", [128, 16 * 512], BF16)
        qstage = sb("qstage", [64, 128])
        qb = sb("qb", [64, 128], BF16)
        tauc = sb("tauc", [128, 1])
        tau11 = sb("tau11", [1, 1])
        chsum = sb("chsum", [1, 512])
        tails = sb("tails", [1, 512])
        tailsb = sb("tailsb", [1, 512], BF16)
        tmpc = sb("tmpc", [128, 1])
        tmpc2 = sb("tmpc2", [128, 1])
        taucol = sb("taucol", [128, 1])
        sumc = sb("sumc", [128, 1])
        rcol = sb("rcol", [128, 1])
        tr1 = sb("tr1", [1, 128])
        tr2 = sb("tr2", [1, 128])
        trrb = sb("trrb", [1, 128], BF16)
        erow = sb("erow", [128, T], BF16)
        sc32 = sb("sc32", [128, 512])
        abf = sb("abf", [128, T], BF16)
        aTb = sb("aTb", [128, 128], BF16)
        yt = sb("yt", [128, D])
        sq2 = sb("sq2", [128, D])
        xtt = sb("xtt", [128, D])
        oTb = sb("oTb", [128, D], BF16)
        col = sb("col", [128, 1])
        col2 = sb("col2", [128, 1])

        pS = ps("pS", [128, 512])
        pTb = ps("pTb", [128, 128], BF16)
        pOh = ps("pOh", [128, 512])
        pTl = ps("pTl", [1, 512])
        pF = ps("pF", [128, 128])
        pW = ps("pW", [128, 512])

        st = {"k_used": [], "v_used": [], "bt_used": [], "q_used": [], "er_used": [],
              "ab_used": [], "sc_used": [], "tau_used": [], "tails_used": [],
              "ps_free": [], "poh_free": [], "ptb_free": [], "ptl_free": [],
              "sum_used": [], "trr_used": []}
        oacc_toks = {s: [oz[s]] for s in range(nslots)}

        for h in range(nheads):
            tl_tau = P.emit("sp", lambda q, h=h: q.dma_start(
                out=tauc[:, :], in_=tau16[h:h + 1].partition_broadcast(128)),
                deps=st["tau_used"], dma=True)
            tl_t11 = P.emit("sp", lambda q, h=h: q.dma_start(
                out=tau11[:, :], in_=tau16[h:h + 1].unsqueeze(0)),
                deps=st["tau_used"], dma=True)
            tk = P.emit("sp", lambda q, h=h: q.dma_start(
                out=kstage[:, :], in_=kTs[h * 64:(h + 1) * 64, :]),
                deps=st["k_used"], dma=True)
            tkc = P.emit("pool", lambda q: q.tensor_copy(out=kb[:, :], in_=kstage[:, :]),
                         deps=[tk] + st["k_used"])
            tv = P.emit("sp", lambda q, h=h: q.dma_start(
                out=vstage[:, :],
                in_=Vs.rearrange("(a p) c -> p a c", p=128)[:, :, h * 64:(h + 1) * 64]),
                deps=st["v_used"], dma=True)
            tvc = P.emit("pool", lambda q: q.tensor_copy(out=vb[:, :], in_=vstage[:, :]),
                         deps=[tv] + st["v_used"])
            bt_toks = []
            btfree = []
            for par in range(2):
                for u in range(8):
                    tb1 = P.emit("sp", lambda q, par=par, u=u, h=h: q.dma_start(
                        out=btstage[:, :], in_=btiles[h, par * 8 + u]),
                        deps=btfree, dma=True)
                    tb2 = P.emit("pool", lambda q, par=par, u=u: q.tensor_copy(
                        out=btb[:, (par * 8 + u) * 512:(par * 8 + u + 1) * 512],
                        in_=btstage[:, :]), deps=[tb1] + st["bt_used"])
                    btfree = [tb2]
                    bt_toks.append(tb2)
            tz = P.emit("dve", lambda q: q.memset(tails[:, :], 0.0), deps=st["tails_used"])
            ch_toks = {}
            ptl_free = st["ptl_free"]
            for ch in range(1, 8):
                last = None
                for blk in range(4):
                    dd = ([tvc, c3] + ptl_free) if blk == 0 else []
                    last = P.emit("pe", lambda q, ch=ch, blk=blk: q.matmul(
                        pTl[:, 0:64], onesb[:, :], vb[:, (4 * ch + blk) * 64:(4 * ch + blk + 1) * 64],
                        start=(blk == 0), stop=(blk == 3)), deps=dd)
                ev = P.emit("dve", lambda q, ch=ch: q.tensor_copy(
                    out=chsum[0:1, ch * 64:(ch + 1) * 64], in_=pTl[0:1, 0:64]), deps=[last])
                ptl_free = [ev]
                ch_toks[ch] = ev
            st["ptl_free"] = ptl_free
            tail_toks = {7: tz}
            for u in range(6, -1, -1):
                tail_toks[u] = P.emit("dve", lambda q, u=u: q.tensor_add(
                    tails[0:1, u * 64:(u + 1) * 64], tails[0:1, (u + 1) * 64:(u + 2) * 64],
                    chsum[0:1, (u + 1) * 64:(u + 2) * 64]),
                    deps=[tail_toks[u + 1], ch_toks[u + 1], tz])
            tcb = P.emit("pool", lambda q: q.tensor_copy(out=tailsb[:, :], in_=tails[:, :]),
                         deps=[tail_toks[0]] + st["tails_used"])

            for s in range(nslots):
                par = s % 2
                us = s // 2
                nch = us + 1
                tq = P.emit("sp", lambda q, h=h, s=s: q.dma_start(
                    out=qstage[:, :], in_=qTs[h * 64:(h + 1) * 64, s * 128:(s + 1) * 128]),
                    deps=st["q_used"], dma=True)
                tqc = P.emit("pool", lambda q: q.tensor_copy(out=qb[:, :], in_=qstage[:, :]),
                            deps=[tq] + st["q_used"])
                u1 = P.emit("dve", lambda q, us=us: q.tensor_scalar_add(
                    tmpc[:, :], iotac[:, :], float(512 * us + 1)), deps=[c4] + st["sum_used"])
                u2 = P.emit("dve", lambda q, par=par: q.tensor_add(
                    tmpc[:, :], tmpc[:, :], roffc[:, par:par + 1]), deps=[u1, c2])
                u3 = P.emit("dve", lambda q: q.reciprocal(tmpc2[:, :], tmpc[:, :]), deps=[u2])
                u4 = P.emit("dve", lambda q: q.tensor_mul(taucol[:, :], tmpc2[:, :], tauc[:, :]),
                            deps=[u3, tl_tau] + st["tau_used2"] if "tau_used2" in st else [u3, tl_tau])
                r1 = P.emit("dve", lambda q, us=us: q.tensor_scalar_add(
                    tr1[:, :], iotar[:, :], float(512 * us + 1)), deps=[c5] + st["trr_used"])
                r2 = P.emit("dve", lambda q, par=par: q.tensor_scalar_add(
                    tr1[:, :], tr1[:, :], roffc[0:1, par:par + 1]), deps=[r1])
                r3 = P.emit("dve", lambda q: q.reciprocal(tr2[:, :], tr1[:, :]), deps=[r2])
                r4 = P.emit("dve", lambda q: q.tensor_single_scalar(
                    tr2[:, :], tr2[:, :], tau11[0:1, 0:1], ALU.mult), deps=[r3, tl_t11])
                r5 = P.emit("dve", lambda q: q.tensor_scalar_max(tr2[:, :], tr2[:, :], 0.0),
                            deps=[r4])
                r6 = P.emit("pool", lambda q: q.tensor_copy(out=trrb[:, :], in_=tr2[:, :]),
                            deps=[r5] + st["trr_used"])
                sum_tok = None
                exp_toks = []
                ps_free = st["ps_free"]
                for c in range(nch):
                    mm = P.emit("pe", lambda q, c=c: q.matmul(
                        pS[:, :], qb[:, :], kb[:, c * 512:(c + 1) * 512],
                        start=True, stop=True),
                        deps=[tqc, tkc] + ps_free)
                    ut = us - c
                    ba = P.emit("dve", lambda q, c=c, par=par, ut=ut: q.tensor_add(
                        sc32[:, :], pS[:, :], btb[:, (par * 8 + ut) * 512:(par * 8 + ut + 1) * 512]),
                        deps=[mm] + bt_toks + st["sc_used"])
                    ps_free = [ba]
                    ex = P.emit("act", lambda q, c=c: q.activation(
                        erow[:, c * 512:(c + 1) * 512], sc32[:, :], AF.Exp),
                        deps=[ba] + st["er_used"])
                    st["sc_used"] = [ex]
                    exp_toks.append(ex)
                    rs = P.emit("dve", lambda q, c=c: q.reduce_sum(
                        tmpc[:, :], erow[:, c * 512:(c + 1) * 512], axis=AX.X), deps=[ex, u2])
                    if c == 0:
                        sum_tok = P.emit("dve", lambda q: q.tensor_copy(out=sumc[:, :], in_=tmpc[:, :]),
                                         deps=[rs])
                    else:
                        sum_tok = P.emit("dve", lambda q: q.tensor_add(sumc[:, :], sumc[:, :], tmpc[:, :]),
                                         deps=[rs, sum_tok])
                st["ps_free"] = ps_free
                st["sum_used"] = [sum_tok]
                rc = P.emit("dve", lambda q: q.reciprocal(rcol[:, :], sumc[:, :]), deps=[sum_tok])
                poh_free = st["poh_free"]
                ptb_free = st["ptb_free"]
                av_last = None
                rl = None
                for c in range(nch):
                    at = P.emit("dve", lambda q, c=c: q.tensor_scalar(
                        sc32[:, :], erow[:, c * 512:(c + 1) * 512], rcol[:, :], taucol[:, :],
                        ALU.mult, ALU.add),
                        deps=[rc, u4, exp_toks[c]] + st["sc_used"] + st["ab_used"] +
                             ([rl] if rl else []))
                    st["sc_used"] = [at]
                    rl = P.emit("act", lambda q, c=c: q.activation(
                        abf[:, c * 512:(c + 1) * 512], sc32[:, :], AF.Relu), deps=[at])
                    for jb in range(4):
                        tt = P.emit("pe", lambda q, c=c, jb=jb: q.transpose(
                            pTb[:, :], abf[:, (c * 4 + jb) * 128:(c * 4 + jb + 1) * 128], idb[:, :]),
                            deps=[rl, c1] + ptb_free)
                        tev = P.emit("dve", lambda q: q.tensor_copy(out=aTb[:, :], in_=pTb[:, :]),
                                     deps=[tt])
                        ptb_free = [tev]
                        av_last = P.emit("pe", lambda q, c=c, jb=jb, nch=nch: q.matmul(
                            pOh[:, 0:64], aTb[:, :], vb[:, (c * 4 + jb) * 64:(c * 4 + jb + 1) * 64],
                            start=(c == 0 and jb == 0), stop=False),
                            deps=[tev, tvc] + (poh_free if (c == 0 and jb == 0) else []))
                tmm = P.emit("pe", lambda q, us=us: q.matmul(
                    pOh[:, 0:64], trrb[:, :], tailsb[0:1, us * 64:(us + 1) * 64],
                    start=False, stop=True), deps=[r6, tcb])
                st["trr_used"] = [tmm]
                st["ab_used"] = [tmm]
                st["sc_used"] = [at, rl]
                oa = P.emit("dve", lambda q, s=s, h=h: q.tensor_add(
                    Oacc[s][:, h * 64:(h + 1) * 64], Oacc[s][:, h * 64:(h + 1) * 64], pOh[:, 0:64]),
                    deps=[tmm] + oacc_toks[s])
                oacc_toks[s] = [oa]
                st["poh_free"] = [oa]
                st["ptb_free"] = ptb_free
                st["q_used"] = [av_last if av_last else tmm, mm]
                st["er_used"] = [at]
                st["tau_used2"] = [at]
            st["k_used"] = [mm]
            st["v_used"] = [av_last, last]
            st["bt_used"] = [ba]
            st["tau_used"] = [u4, r4]
            st["tails_used"] = [tmm]
        ofree = []
        yfree = []
        xfree = []
        pw_free = []
        tprev = []
        for s in range(nslots):
            tx = P.emit("sp", lambda q, s=s: q.dma_start(
                out=xtt[:, :], in_=x2sl[s * 128:(s + 1) * 128, :]), deps=xfree, dma=True)
            ttoks = []
            for j in range(8):
                tt = P.emit("pe", lambda q, s=s, j=j: q.transpose(
                    pF[:, :], Oacc[s][:, j * 128:(j + 1) * 128], idf[:, :]),
                    deps=oacc_toks[s] + tprev)
                te = P.emit("dve", lambda q, j=j: q.tensor_copy(
                    out=oTb[:, j * 128:(j + 1) * 128], in_=pF[:, :]), deps=[tt] + ofree)
                tprev = [te]
                ttoks.append(te)
            ofree = []
            mm2 = [None, None]
            for half in range(2):
                last2 = None
                for j in range(8):
                    dd = []
                    if j == 0:
                        dd = ttoks + wo_toks + pw_free
                    last2 = P.emit("pe", lambda q, half=half, j=j: q.matmul(
                        pW[:, :], oTb[:, j * 128:(j + 1) * 128],
                        wotb[j][:, half * 512:(half + 1) * 512],
                        start=(j == 0), stop=(j == 7)), deps=dd)
                e1 = P.emit("dve", lambda q, half=half: q.tensor_add(
                    yt[:, half * 512:(half + 1) * 512], xtt[:, half * 512:(half + 1) * 512],
                    pW[:, :]), deps=[last2, tx] + yfree)
                pw_free = [e1]
                mm2[half] = e1
            ofree = [mm2[0], mm2[1]]
            l1 = P.emit("dve", lambda q: q.reduce_sum(col[:, :], yt[:, :], axis=AX.X), deps=mm2)
            l2_ = P.emit("dve", lambda q: q.tensor_scalar_mul(col2[:, :], col[:, :], 1.0 / D), deps=[l1])
            l3 = P.emit("dve", lambda q: q.tensor_scalar_sub(yt[:, :], yt[:, :], col2[:, :]), deps=[l2_])
            l4 = P.emit("act", lambda q: q.square(sq2[:, :], yt[:, :]), deps=[l3])
            l5 = P.emit("dve", lambda q: q.reduce_sum(col[:, :], sq2[:, :], axis=AX.X), deps=[l4])
            l6 = P.emit("dve", lambda q: q.tensor_scalar(col2[:, :], col[:, :], 1.0 / D, 1e-5,
                                                         ALU.mult, ALU.add), deps=[l5])
            l7 = P.emit("act", lambda q: q.sqrt(col[:, :], col2[:, :]), deps=[l6])
            l8 = P.emit("dve", lambda q: q.reciprocal(col2[:, :], col[:, :]), deps=[l7])
            l9 = P.emit("dve", lambda q: q.tensor_scalar_mul(yt[:, :], yt[:, :], col2[:, :]), deps=[l8])
            tw = P.emit("pool", lambda q, s=s: q.dma_start(
                out=x3sl[s * 128:(s + 1) * 128, :], in_=yt[:, :]), deps=[l9], dma=True)
            yfree = [tw, l9]
            xfree = mm2

        P.barrier()
        P.finalize()
    return nc


def _slot_T(s, g):
    r = [[0, 3], [1, 2]][g][s % 2]
    return 4 * (s // 2) + r


def run_launch6(inputs, x2, qT, kT, V):
    nc = build_launch6(new_nc())
    ident = _ident128()
    bias = inputs["L1_attn_bias"]
    brows = []
    for r in range(4):
        d = 4096 + 128 * r - np.arange(4608)
        row = np.zeros((16, 4608), np.float32)
        row[:, :] = np.where(d < 0, np.float32(-1e30), 0.0)[None, :]
        m = (d >= 0) & (d < 2048)
        row[:, m] = bias[:, d[m]]
        brows.append(row)
    woT = np.ascontiguousarray(inputs["L1_attn_Wo"].T)
    pidx = np.arange(128)[:, None]
    fidx = np.arange(512)[None, :]
    btile_r = {}
    for r in range(4):
        bt = np.zeros((16, 8, 128, 512), np.float32)
        for u in range(8):
            J0 = 4096 - 512 * u
            bt[:, u] = brows[r][:, (J0 - pidx + fidx).ravel()].reshape(16, 128, 512)
        btile_r[r] = bt
    in_maps = []
    for c in range(NCORES):
        b, g = c // 2, c % 2
        qTs = np.zeros((D, TOK), np.float32)
        x2sl = np.zeros((TOK, D), np.float32)
        for s in range(16):
            Tt = _slot_T(s, g)
            qTs[:, s * 128:(s + 1) * 128] = qT[b][:, Tt * 128:(Tt + 1) * 128]
            x2sl[s * 128:(s + 1) * 128, :] = x2[b, Tt * 128:(Tt + 1) * 128, :]
        r_even = [[0, 3], [1, 2]][g][0]
        r_odd = [[0, 3], [1, 2]][g][1]
        btl = np.concatenate([btile_r[r_even], btile_r[r_odd]], axis=1)
        in_maps.append({
            "qTs": qTs, "kTs": np.ascontiguousarray(kT[b]),
            "Vs": np.ascontiguousarray(V[b]),
            "btiles": np.ascontiguousarray(btl),
            "tau16": inputs["L1_attn_tau"],
            "roffs": np.array([128.0 * r_even, 128.0 * r_odd], np.float32),
            "x2sl": x2sl, "woT": woT, "ident": ident,
        })
    res = run_bass_kernel_spmd(nc, in_maps, core_ids=list(range(NCORES)))
    x3 = np.zeros((B, T, D), np.float32)
    for c in range(NCORES):
        b, g = c // 2, c % 2
        r = res.results[c]["x3sl"]
        for s in range(16):
            Tt = _slot_T(s, g)
            x3[b, Tt * 128:(Tt + 1) * 128, :] = r[s * 128:(s + 1) * 128, :]
    return x3



def build_launch7a(nc):
    x3h = nc.dram_tensor("x3sh", [TOK, D], F32, kind="ExternalInput").ap()
    ipwT = nc.dram_tensor("ipwT", [D, 3072], F32, kind="ExternalInput").ap()
    ipb = nc.dram_tensor("ipb", [3072], F32, kind="ExternalInput").ap()
    opwT = nc.dram_tensor("opwT", [D, D], F32, kind="ExternalInput").ap()
    opb = nc.dram_tensor("opb", [D], F32, kind="ExternalInput").ap()
    ln1g = nc.dram_tensor("ln1g", [D], F32, kind="ExternalInput").ap()
    ln1b = nc.dram_tensor("ln1b", [D], F32, kind="ExternalInput").ap()
    wg = nc.dram_tensor("wg", [D, 8], F32, kind="ExternalInput").ap()
    mham = nc.dram_tensor("mham", [128, 128], F32, kind="ExternalInput").ap()
    ident = nc.dram_tensor("ident", [128, 128], F32, kind="ExternalInput").ap()
    y1h = nc.dram_tensor("y1sh", [TOK, D], F32, kind="ExternalOutput").ap()
    gh = nc.dram_tensor("gsh", [TOK, 8], F32, kind="ExternalOutput").ap()

    NT = TOK // 128
    with ExitStack() as stk:
        P = Prog(nc, stk)
        sb = lambda n, s, d=F32: stk.enter_context(nc.sbuf_tensor(n, s, d))
        ps = lambda n, s, d=F32: stk.enter_context(nc.psum_tensor(n, s, d))
        idf = sb("idf", [128, 128])
        idb = sb("idb", [128, 128], BF16)
        c0 = P.emit("sp", lambda q: q.dma_start(out=idf[:, :], in_=ident[:, :]), dma=True)
        c1 = P.emit("pool", lambda q: q.tensor_copy(out=idb[:, :], in_=idf[:, :]), deps=[c0])
        mhb = sb("mhb", [128, 128])
        c2 = P.emit("sp", lambda q: q.dma_start(out=mhb[:, :], in_=mham[:, :]), dma=True)
        opbc = sb("opbc", [128, D])
        c3 = P.emit("sp", lambda q: q.dma_start(out=opbc[:, :], in_=opb.partition_broadcast(128)), dma=True)
        g1bc = sb("g1bc", [128, D])
        c4 = P.emit("sp", lambda q: q.dma_start(out=g1bc[:, :], in_=ln1g.partition_broadcast(128)), dma=True)
        b1bc = sb("b1bc", [128, D])
        c5 = P.emit("sp", lambda q: q.dma_start(out=b1bc[:, :], in_=ln1b.partition_broadcast(128)), dma=True)
        ipbc = sb("ipbc", [128, 24])
        cb = []
        for o in range(24):
            cb.append(P.emit("sp", lambda q, o=o: q.dma_start(
                out=ipbc[:, o:o + 1], in_=ipb[o * 128:(o + 1) * 128].unsqueeze(1)), dma=True))
        wgf = [sb(f"wgf{j}", [128, 8]) for j in range(8)]
        cw = []
        for j in range(8):
            cw.append(P.emit("sp", lambda q, j=j: q.dma_start(
                out=wgf[j][:, :], in_=wg[j * 128:(j + 1) * 128, :]), dma=True))

        x3Tb = [sb(f"x3Tb{j}", [128, TOK], BF16) for j in range(8)]
        xt = sb("xt", [128, D])
        pF = ps("pF", [128, 128])
        xa = []
        tprev = []
        prev_e = []
        for i in range(NT):
            tl = P.emit("sp", lambda q, i=i: q.dma_start(
                out=xt[:, :], in_=x3h[i * 128:(i + 1) * 128, :]), deps=prev_e, dma=True)
            prev_e = []
            for j in range(8):
                tt = P.emit("pe", lambda q, j=j: q.transpose(
                    pF[:, :], xt[:, j * 128:(j + 1) * 128], idf[:, :]), deps=[tl, c0] + tprev)
                te = P.emit("dve", lambda q, i=i, j=j: q.tensor_copy(
                    out=x3Tb[j][:, i * 128:(i + 1) * 128], in_=pF[:, :]), deps=[tt])
                tprev = [te]
                prev_e = [te]
            xa.append(prev_e[0])

        wstage = sb("wstage", [128, 3072])
        ipwb = [sb(f"ipwb{j}", [128, 3072], BF16) for j in range(8)]
        opwb = [sb(f"opwb{j}", [128, 1024], BF16) for j in range(8)]
        wt_ip, wt_op = [], []
        sfree = []
        for j in range(8):
            tl = P.emit("sp", lambda q, j=j: q.dma_start(
                out=wstage[:, :], in_=ipwT[j * 128:(j + 1) * 128, :]), deps=sfree, dma=True)
            tc = P.emit("pool", lambda q, j=j: q.tensor_copy(out=ipwb[j][:, :], in_=wstage[:, :]),
                        deps=[tl])
            sfree = [tc]
            wt_ip.append(tc)
        for j in range(8):
            tl = P.emit("sp", lambda q, j=j: q.dma_start(
                out=wstage[:, 0:1024], in_=opwT[j * 128:(j + 1) * 128, :]), deps=sfree, dma=True)
            tc = P.emit("pool", lambda q, j=j: q.tensor_copy(out=opwb[j][:, :], in_=wstage[:, 0:1024]),
                        deps=[tl])
            sfree = [tc]
            wt_op.append(tc)

        qkvb = [sb(f"qkvb{o}", [128, 512], BF16) for o in range(24)]
        vN = [sb(f"vN{b4}", [128, D], BF16) for b4 in range(4)]
        s32 = sb("s32", [128, 128])
        e32 = sb("e32", [128, 128])
        ssum = sb("ssum", [128, 32])
        abf = sb("abf", [128, 128], BF16)
        aT = sb("aT", [128, 128], BF16)
        Ob = sb("Ob", [128, D])
        oTb = sb("oTb", [128, D], BF16)
        y1T = sb("y1T", [128, 128])
        xa2 = sb("xa2", [128, D])
        yt = sb("yt", [128, D])
        sq2 = sb("sq2", [128, D])
        gT = sb("gT", [8, 128])
        gn = sb("gn", [128, 8])
        col = sb("col", [128, 1])
        col2 = sb("col2", [128, 1])
        pP = [ps(f"pP{u}", [128, 512]) for u in range(2)]
        pS = ps("pS", [128, 512])
        pTb = ps("pTb", [128, 128], BF16)
        pOv = [ps(f"pOv{u}", [128, 512]) for u in range(2)]
        pW = ps("pW", [128, 512])

        st = {"qkv_used": [], "pp_free": [[], []], "ps_free": [], "ptb_free": [],
              "pov_free": [[], []], "vn_used": [], "ob_used": [], "otb_used": [],
              "pw_free": [], "yt_used": [], "xa2_used": [], "y1t_used": [], "pf_used": tprev,
              "s32_used": [], "e32_used": [], "abf_used": [], "at_used": [], "gn_used": [],
              "gt_used": [], "ssum_used": []}

        for ch in range(4):
            ev_all = []
            for o in range(24):
                u = o % 2
                last = None
                for e in range(8):
                    dd = []
                    if e == 0:
                        dd = wt_ip + xa + st["pp_free"][u] + st["qkv_used"]
                    last = P.emit("pe", lambda q, o=o, e=e, ch=ch: q.matmul(
                        pP[o % 2][:, :], ipwb[e][:, o * 128:(o + 1) * 128],
                        x3Tb[e][:, ch * 512:(ch + 1) * 512],
                        start=(e == 0), stop=(e == 7)), deps=dd)
                sc = 0.0625 if o < 8 else 1.0
                evt = P.emit("dve", lambda q, o=o, sc=sc: q.tensor_scalar(
                    qkvb[o][:, :], pP[o % 2][:, :], ipbc[:, o:o + 1], sc, ALU.add, ALU.mult),
                    deps=[last] + cb)
                st["pp_free"][u] = [evt]
                ev_all.append(evt)
            st["qkv_used"] = []
            vn_toks = []
            tprev2 = st["ptb_free"]
            for j in range(8):
                for b4 in range(4):
                    tt = P.emit("pe", lambda q, j=j, b4=b4: q.transpose(
                        pTb[:, :], qkvb[16 + j][:, b4 * 128:(b4 + 1) * 128], idb[:, :]),
                        deps=[ev_all[16 + j], c1] + tprev2 + st["vn_used"])
                    te = P.emit("dve", lambda q, j=j, b4=b4: q.tensor_copy(
                        out=vN[b4][:, j * 128:(j + 1) * 128], in_=pTb[:, :]), deps=[tt])
                    tprev2 = [te]
                    vn_toks.append(te)
            st["ptb_free"] = tprev2
            st["vn_used"] = []
            for b4 in range(4):
                i = ch * 4 + b4
                ob_toks = []
                for h in range(4):
                    u = h % 2
                    P.emit("pe", lambda q, h=h, b4=b4: q.matmul(
                        pS[:, 0:128], qkvb[2 * h][:, b4 * 128:(b4 + 1) * 128],
                        qkvb[8 + 2 * h][:, b4 * 128:(b4 + 1) * 128],
                        start=True, stop=False),
                        deps=[ev_all[2 * h], ev_all[2 * h + 1], ev_all[8 + 2 * h],
                              ev_all[9 + 2 * h]] + st["ps_free"])
                    mm = P.emit("pe", lambda q, h=h, b4=b4: q.matmul(
                        pS[:, 0:128], qkvb[2 * h + 1][:, b4 * 128:(b4 + 1) * 128],
                        qkvb[9 + 2 * h][:, b4 * 128:(b4 + 1) * 128],
                        start=False, stop=True), deps=[])
                    a1 = P.emit("dve", lambda q: q.tensor_add(s32[:, :], pS[:, 0:128], mhb[:, :]),
                                deps=[mm, c2] + st["s32_used"])
                    st["ps_free"] = [a1]
                    a2 = P.emit("act", lambda q: q.activation(e32[:, :], s32[:, :], AF.Exp),
                                deps=[a1] + st["e32_used"])
                    st["s32_used"] = [a2]
                    a3 = P.emit("dve", lambda q: q.reduce_sum(
                        ssum[:, :].rearrange("p (a e) -> p a e", e=1),
                        e32[:, :].rearrange("p (a e) -> p a e", e=4), axis=AX.X),
                        deps=[a2] + st["ssum_used"])
                    a3b = P.emit("dve", lambda q: q.tensor_scalar_add(ssum[:, :], ssum[:, :], 1e-30),
                                 deps=[a3])
                    a4 = P.emit("dve", lambda q: q.reciprocal(ssum[:, :], ssum[:, :]), deps=[a3b])
                    a5 = P.emit("dve", lambda q: q.tensor_tensor(
                        abf[:, :].rearrange("p (a e) -> p a e", e=4),
                        e32[:, :].rearrange("p (a e) -> p a e", e=4),
                        ssum[:, :].rearrange("p (a e) -> p a e", e=1).broadcast_to((128, 32, 4)),
                        ALU.mult), deps=[a4] + st["abf_used"])
                    st["e32_used"] = [a5]
                    st["ssum_used"] = [a5]
                    tt = P.emit("pe", lambda q: q.transpose(pTb[:, :], abf[:, :], idb[:, :]),
                                deps=[a5, c1] + st["ptb_free"])
                    tev = P.emit("dve", lambda q: q.tensor_copy(out=aT[:, :], in_=pTb[:, :]),
                                 deps=[tt] + st["at_used"])
                    st["ptb_free"] = [tev]
                    st["abf_used"] = [tt]
                    av = P.emit("pe", lambda q, h=h, b4=b4: q.matmul(
                        pOv[u][:, 0:256], aT[:, :], vN[b4][:, h * 256:(h + 1) * 256],
                        start=True, stop=True),
                        deps=[tev] + vn_toks + st["pov_free"][u])
                    oe = P.emit("dve", lambda q, h=h: q.tensor_copy(
                        out=Ob[:, h * 256:(h + 1) * 256], in_=pOv[u][:, 0:256]),
                        deps=[av] + st["ob_used"])
                    st["pov_free"][u] = [oe]
                    st["at_used"] = [av]
                    ob_toks.append(oe)
                st["ob_used"] = []
                st["vn_used"] += [av]
                ttoks = []
                for j in range(8):
                    tt = P.emit("pe", lambda q, j=j: q.transpose(
                        pF[:, :], Ob[:, j * 128:(j + 1) * 128], idf[:, :]),
                        deps=ob_toks + st["pf_used"])
                    te = P.emit("dve", lambda q, j=j: q.tensor_copy(
                        out=oTb[:, j * 128:(j + 1) * 128], in_=pF[:, :]),
                        deps=[tt] + st["otb_used"])
                    st["pf_used"] = [te]
                    ttoks.append(te)
                st["ob_used"] = [ttoks[-1]]
                tx = P.emit("sp", lambda q, i=i: q.dma_start(
                    out=xa2[:, :], in_=x3h[i * 128:(i + 1) * 128, :]),
                    deps=st["xa2_used"], dma=True)
                mm2 = []
                for half in range(2):
                    last2 = None
                    for j in range(8):
                        dd = []
                        if j == 0:
                            dd = ttoks + wt_op + st["pw_free"]
                        last2 = P.emit("pe", lambda q, half=half, j=j: q.matmul(
                            pW[:, :], oTb[:, j * 128:(j + 1) * 128],
                            opwb[j][:, half * 512:(half + 1) * 512],
                            start=(j == 0), stop=(j == 7)), deps=dd)
                    e1 = P.emit("dve", lambda q, half=half: q.tensor_add(
                        yt[:, half * 512:(half + 1) * 512], pW[:, :],
                        opbc[:, half * 512:(half + 1) * 512]),
                        deps=[last2, c3] + st["yt_used"])
                    e2 = P.emit("dve", lambda q, half=half: q.tensor_add(
                        yt[:, half * 512:(half + 1) * 512], yt[:, half * 512:(half + 1) * 512],
                        xa2[:, half * 512:(half + 1) * 512]), deps=[e1, tx])
                    st["pw_free"] = [e1]
                    mm2.append(e2)
                st["otb_used"] = mm2
                l1 = P.emit("dve", lambda q: q.reduce_sum(col[:, :], yt[:, :], axis=AX.X), deps=mm2)
                l2_ = P.emit("dve", lambda q: q.tensor_scalar_mul(col2[:, :], col[:, :], 1.0 / D), deps=[l1])
                l3 = P.emit("dve", lambda q: q.tensor_scalar_sub(yt[:, :], yt[:, :], col2[:, :]), deps=[l2_])
                l4 = P.emit("act", lambda q: q.square(sq2[:, :], yt[:, :]), deps=[l3])
                l5 = P.emit("dve", lambda q: q.reduce_sum(col[:, :], sq2[:, :], axis=AX.X), deps=[l4])
                l6 = P.emit("dve", lambda q: q.tensor_scalar(col2[:, :], col[:, :], 1.0 / D, 1e-5,
                                                             ALU.mult, ALU.add), deps=[l5])
                l7 = P.emit("act", lambda q: q.sqrt(col[:, :], col2[:, :]), deps=[l6])
                l8 = P.emit("dve", lambda q: q.reciprocal(col2[:, :], col[:, :]), deps=[l7])
                l9 = P.emit("dve", lambda q: q.scalar_tensor_tensor(
                    yt[:, :], yt[:, :], col2[:, :], g1bc[:, :], ALU.mult, ALU.mult),
                    deps=[l8, c4])
                l10 = P.emit("dve", lambda q: q.tensor_add(yt[:, :], yt[:, :], b1bc[:, :]),
                             deps=[l9, c5])
                tw = P.emit("pool", lambda q, i=i: q.dma_start(
                    out=y1h[i * 128:(i + 1) * 128, :], in_=yt[:, :]), deps=[l10], dma=True)
                gmm = None
                gp = []
                for j in range(8):
                    tt = P.emit("pe", lambda q, j=j: q.transpose(
                        pF[:, :], yt[:, j * 128:(j + 1) * 128], idf[:, :]),
                        deps=[l10] + st["pf_used"])
                    te = P.emit("act", lambda q: q.copy(y1T[:, :], pF[:, :]),
                                deps=[tt] + st["y1t_used"])
                    st["pf_used"] = [te]
                    gmm = P.emit("pe", lambda q, j=j: q.matmul(
                        pS[0:8, 0:128], wgf[j][:, :], y1T[:, :],
                        start=(j == 0), stop=(j == 7)),
                        deps=[te] + cw + (st["ps_free"] if j == 0 else []))
                    st["y1t_used"] = [gmm]
                ge = P.emit("dve", lambda q: q.tensor_copy(out=gT[:, :], in_=pS[0:8, 0:128]),
                            deps=[gmm] + st["gt_used"])
                st["ps_free"] = [ge]
                tt = P.emit("pe", lambda q: q.transpose(pF[0:128, 0:8], gT[:, :], idf[0:8, 0:8]),
                            deps=[ge] + st["pf_used"])
                gt2 = P.emit("dve", lambda q: q.tensor_copy(out=gn[:, :], in_=pF[0:128, 0:8]),
                             deps=[tt] + st["gn_used"])
                st["pf_used"] = [gt2]
                st["gt_used"] = [tt]
                g1 = P.emit("dve", lambda q: q.reduce_max(col[:, :], gn[:, :], axis=AX.X), deps=[gt2])
                g2 = P.emit("dve", lambda q: q.tensor_scalar_mul(col2[:, :], col[:, :], -1.0), deps=[g1])
                g3 = P.emit("act", lambda q: q.activation(gn[:, :], gn[:, :], AF.Exp,
                                                          bias=col2[:, :], scale=1.0), deps=[g2])
                g4 = P.emit("dve", lambda q: q.reduce_sum(col[:, :], gn[:, :], axis=AX.X), deps=[g3])
                g5 = P.emit("dve", lambda q: q.reciprocal(col2[:, :], col[:, :]), deps=[g4])
                g6 = P.emit("dve", lambda q: q.tensor_scalar_mul(gn[:, :], gn[:, :], col2[:, :]),
                            deps=[g5])
                gw_ = P.emit("pool", lambda q, i=i: q.dma_start(
                    out=gh[i * 128:(i + 1) * 128, :], in_=gn[:, :]), deps=[g6], dma=True)
                st["gn_used"] = [g6, gw_]
                st["yt_used"] = [tw, gmm, l10]
                st["xa2_used"] = mm2
            st["qkv_used"] = [mm, av]

        P.barrier()
        P.finalize()
    return nc


def run_launch7a(inputs, x3):
    nc = build_launch7a(new_nc())
    ident = _ident128()
    mham = np.full((128, 128), -1e30, np.float32)
    for n in range(32):
        mham[n * 4:(n + 1) * 4, n * 4:(n + 1) * 4] = 0.0
    in_maps = []
    for c in range(NCORES):
        xs = np.ascontiguousarray(
            x3[:, c * NPC:(c + 1) * NPC, :].transpose(1, 0, 2).reshape(TOK, D))
        in_maps.append({
            "x3sh": xs,
            "ipwT": np.ascontiguousarray(inputs["L1_ffn_in_proj_w"].T),
            "ipb": inputs["L1_ffn_in_proj_b"],
            "opwT": np.ascontiguousarray(inputs["L1_ffn_out_proj_w"].T),
            "opb": inputs["L1_ffn_out_proj_b"],
            "ln1g": inputs["L1_ffn_ln1_g"], "ln1b": inputs["L1_ffn_ln1_b"],
            "wg": inputs["L1_ffn_wg"], "mham": mham, "ident": ident,
        })
    res = run_bass_kernel_spmd(nc, in_maps, core_ids=list(range(NCORES)))
    y1 = np.zeros((4, T, D), np.float32)
    gates = np.zeros((4, T, 8), np.float32)
    for c in range(NCORES):
        r = res.results[c]
        y1[:, c * NPC:(c + 1) * NPC, :] = r["y1sh"].reshape(NPC, 4, D).transpose(1, 0, 2)
        gates[:, c * NPC:(c + 1) * NPC, :] = r["gsh"].reshape(NPC, 4, 8).transpose(1, 0, 2)
    return y1, gates



def build_launch7b(nc):
    FF = 4096
    w1T = nc.dram_tensor("w1T", [FF, D], F32, kind="ExternalInput").ap()
    w2 = nc.dram_tensor("w2e", [FF, D], F32, kind="ExternalInput").ap()
    b1 = nc.dram_tensor("b1e", [FF], F32, kind="ExternalInput").ap()
    b2 = nc.dram_tensor("b2e", [D], F32, kind="ExternalInput").ap()
    Me = nc.dram_tensor("Me", [D, D], F32, kind="ExternalOutput").ap()
    ce = nc.dram_tensor("ce", [1, D], F32, kind="ExternalOutput").ap()

    with ExitStack() as stk:
        P = Prog(nc, stk)
        sb = lambda n, s, d=F32: stk.enter_context(nc.sbuf_tensor(n, s, d))
        ps = lambda n, s, d=F32: stk.enter_context(nc.psum_tensor(n, s, d))
        wstage = sb("wstage", [128, D])
        w1b = [sb(f"w1b{f}", [128, D], BF16) for f in range(32)]
        w2b = [sb(f"w2b{f}", [128, D], BF16) for f in range(32)]
        b1b = [sb(f"b1b{f}", [128, 1], BF16) for f in range(32)]
        b2r = sb("b2r", [1, D])
        crow = sb("crow", [1, D])
        mrow = sb("mrow", [128, 512])
        pM = [ps(f"pM{u}", [128, 512]) for u in range(2)]
        pC = ps("pC", [1, 512])
        t_b2 = P.emit("sp", lambda q: q.dma_start(out=b2r[:, :], in_=b2.unsqueeze(0)), dma=True)
        toks = []
        sfree = []
        for f in range(32):
            tl = P.emit("sp", lambda q, f=f: q.dma_start(
                out=wstage[:, :], in_=w1T[f * 128:(f + 1) * 128, :]), deps=sfree, dma=True)
            tc = P.emit("pool", lambda q, f=f: q.tensor_copy(out=w1b[f][:, :], in_=wstage[:, :]),
                        deps=[tl])
            sfree = [tc]
            toks.append(tc)
            tl = P.emit("sp", lambda q, f=f: q.dma_start(
                out=wstage[:, :], in_=w2[f * 128:(f + 1) * 128, :]), deps=sfree, dma=True)
            tc = P.emit("pool", lambda q, f=f: q.tensor_copy(out=w2b[f][:, :], in_=wstage[:, :]),
                        deps=[tl])
            sfree = [tc]
            toks.append(tc)
            tb = P.emit("sp", lambda q, f=f: q.dma_start(
                out=wstage[:, 0:1], in_=b1[f * 128:(f + 1) * 128].unsqueeze(1)), deps=sfree, dma=True)
            tc = P.emit("pool", lambda q, f=f: q.tensor_copy(out=b1b[f][:, :], in_=wstage[:, 0:1]),
                        deps=[tb])
            sfree = [tc]
            toks.append(tc)
        pm_free = [[], []]
        mrow_free = []
        for dt_ in range(8):
            for half in range(2):
                u = half
                last = None
                for f in range(32):
                    dd = []
                    if f == 0:
                        dd = toks + pm_free[u]
                    last = P.emit("pe", lambda q, dt_=dt_, half=half, f=f: q.matmul(
                        pM[half][:, :], w1b[f][:, dt_ * 128:(dt_ + 1) * 128],
                        w2b[f][:, half * 512:(half + 1) * 512],
                        start=(f == 0), stop=(f == 31)), deps=dd)
                ev = P.emit("dve", lambda q, half=half: q.tensor_copy(
                    out=mrow[:, :], in_=pM[half][:, :]),
                    deps=[last] + pm_free[u] + mrow_free)
                tw = P.emit("pool", lambda q, dt_=dt_, half=half: q.dma_start(
                    out=Me[dt_ * 128:(dt_ + 1) * 128, half * 512:(half + 1) * 512],
                    in_=mrow[:, :]), deps=[ev], dma=True)
                mrow_free = [tw]
                pm_free[u] = [ev, tw]
        pcf = []
        for half in range(2):
            last = None
            for f in range(32):
                dd = []
                if f == 0:
                    dd = toks + pcf
                last = P.emit("pe", lambda q, half=half, f=f: q.matmul(
                    pC[:, :], b1b[f][:, :], w2b[f][:, half * 512:(half + 1) * 512],
                    start=(f == 0), stop=(f == 31)), deps=dd)
            ev = P.emit("dve", lambda q, half=half: q.tensor_add(
                crow[0:1, half * 512:(half + 1) * 512], pC[0:1, :],
                b2r[0:1, half * 512:(half + 1) * 512]), deps=[last, t_b2] + pcf)
            pcf = [ev]
        P.emit("pool", lambda q: q.dma_start(out=ce[:, :], in_=crow[:, :]), deps=pcf, dma=True)
        P.barrier()
        P.finalize()
    return nc


def run_launch7b(inputs):
    nc = build_launch7b(new_nc())
    w1T = np.ascontiguousarray(inputs["L1_ffn_w1"].transpose(0, 2, 1))
    in_maps = []
    for e in range(8):
        in_maps.append({
            "w1T": w1T[e], "w2e": inputs["L1_ffn_w2"][e],
            "b1e": inputs["L1_ffn_b1"][e], "b2e": inputs["L1_ffn_b2"][e],
        })
    res = run_bass_kernel_spmd(nc, in_maps, core_ids=list(range(NCORES)))
    M = np.stack([res.results[e]["Me"] for e in range(8)])
    cvec = np.stack([res.results[e]["ce"][0] for e in range(8)])
    return M, cvec



def build_launch7c1(nc):
    S = B * T
    gatesG = nc.dram_tensor("gatesG", [S, 8], F32, kind="ExternalInput").ap()
    mtu = nc.dram_tensor("mtu", [128, 128], F32, kind="ExternalInput").ap()
    coefh = nc.dram_tensor("coefh", [S, 8], F32, kind="ExternalOutput").ap()

    with ExitStack() as stk:
        P = Prog(nc, stk)
        sb = lambda n, s, d=F32: stk.enter_context(nc.sbuf_tensor(n, s, d))
        ps = lambda n, s, d=F32: stk.enter_context(nc.psum_tensor(n, s, d))
        mTU = sb("mTU", [128, 128])
        c0 = P.emit("sp", lambda q: q.dma_start(out=mTU[:, :], in_=mtu[:, :]), dma=True)
        gg = sb("gg", [128, 1024])
        c1 = P.emit("sp", lambda q: q.dma_start(
            out=gg[:, :], in_=gatesG.rearrange("(go gi) e -> go (gi e)", gi=128)), dma=True)
        iotaM = sb("iotaM", [128, 1024])
        iotaE = sb("iotaE", [128, 1024])
        c2 = P.emit("pool", lambda q: q.iota(
            iotaM[:, :].rearrange("p (a e) -> p a e", e=8), [[0, 128], [1, 8]],
            base=-1000000, channel_multiplier=0, allow_small_or_imprecise_dtypes=True))
        c3 = P.emit("pool", lambda q: q.iota(
            iotaE[:, :].rearrange("p (a e) -> p a e", e=8), [[0, 128], [1, 8]],
            base=0, channel_multiplier=0, allow_small_or_imprecise_dtypes=True))
        mg = sb("mg", [128, 128])
        oh = sb("oh", [128, 1024])
        tA = sb("tA", [128, 1024])
        tB = sb("tB", [128, 1024])
        idxc = sb("idxc", [128, 128])
        offs = sb("offs", [128, 8])
        locp = sb("locp", [128, 128])
        coefc = sb("coefc", [128, 128])
        pOf = ps("pOf", [128, 512])
        V3 = lambda t: t[:, :].rearrange("p (a e) -> p a e", e=8)
        M3 = lambda t: t[:, :].rearrange("p (a e) -> p a e", e=1)

        t1 = P.emit("dve", lambda q: q.reduce_max(M3(mg), V3(gg), axis=AX.X), deps=[c1])
        t2 = P.emit("dve", lambda q: q.tensor_tensor(
            V3(oh), V3(gg), M3(mg).broadcast_to((128, 128, 8)), ALU.is_ge), deps=[t1])
        t3 = P.emit("dve", lambda q: q.tensor_mul(tA[:, :], oh[:, :], iotaM[:, :]), deps=[t2, c2])
        t4 = P.emit("dve", lambda q: q.tensor_scalar_add(tA[:, :], tA[:, :], 1e6), deps=[t3])
        t5 = P.emit("dve", lambda q: q.tensor_reduce(M3(idxc), V3(tA), AX.X, ALU.min), deps=[t4])
        t6 = P.emit("dve", lambda q: q.tensor_tensor(
            V3(tA), iotaE[:, :].rearrange("p (a e) -> p a e", e=8),
            M3(idxc).broadcast_to((128, 128, 8)), ALU.is_equal), deps=[t5, c3])
        cumA = sb("cumA", [128, 1024])
        cumB = sb("cumB", [128, 1024])
        tB2 = sb("tB2", [128, 1024])
        pa = P.emit("dve", lambda q: q.tensor_copy(out=cumA[:, 0:8], in_=tA[:, 0:8]), deps=[t6])
        pb = P.emit("dve", lambda q: q.tensor_add(cumA[:, 8:1024], tA[:, 8:1024], tA[:, 0:1016]),
                    deps=[pa])
        prev = pb
        cur, oth = cumA, cumB
        for sh in [2, 4, 8, 16, 32, 64]:
            s8 = sh * 8
            pa = P.emit("dve", lambda q, s8=s8, cur=cur, oth=oth: q.tensor_copy(
                out=oth[:, 0:s8], in_=cur[:, 0:s8]), deps=[prev])
            pb = P.emit("dve", lambda q, s8=s8, cur=cur, oth=oth: q.tensor_add(
                oth[:, s8:1024], cur[:, s8:1024], cur[:, 0:1024 - s8]), deps=[pa])
            prev = pb
            cur, oth = oth, cur
        t7 = P.emit("pe", lambda q, cur=cur: q.matmul(
            pOf[:, 0:8], mTU[:, :], cur[:, 1016:1024], start=True, stop=True),
            deps=[prev, c0])
        t8 = P.emit("dve", lambda q: q.tensor_copy(out=offs[:, :], in_=pOf[0:128, 0:8]), deps=[t7])
        t9 = P.emit("dve", lambda q, cur=cur, oth=oth: q.tensor_tensor(
            V3(oth), V3(cur), offs[:, :].unsqueeze(1).broadcast_to((128, 128, 8)), ALU.add),
            deps=[t8, prev])
        full = oth
        t10 = P.emit("dve", lambda q, full=full: q.tensor_mul(tB2[:, :], full[:, :], tA[:, :]),
                     deps=[t9, t6])
        t11 = P.emit("dve", lambda q: q.tensor_reduce(M3(locp), V3(tB2), AX.X, ALU.add),
                     deps=[t10])
        t12 = P.emit("dve", lambda q: q.tensor_single_scalar(locp[:, :], locp[:, :], 2048.0, ALU.is_le),
                     deps=[t11])
        t13 = P.emit("dve", lambda q: q.tensor_mul(coefc[:, :], locp[:, :], mg[:, :]), deps=[t12, t1])
        t14 = P.emit("dve", lambda q: q.tensor_tensor(
            V3(tB2), V3(tA), M3(coefc).broadcast_to((128, 128, 8)), ALU.mult), deps=[t13, t10])
        tw = P.emit("pool", lambda q: q.dma_start(
            out=coefh.rearrange("(go gi) e -> go (gi e)", gi=128),
            in_=tB2[:, :]), deps=[t14], dma=True)
        P.barrier()
        P.finalize()
    return nc



def build_launch7c2(nc):
    y1h = nc.dram_tensor("y1sh", [TOK, D], F32, kind="ExternalInput").ap()
    x3h = nc.dram_tensor("x3sh2", [TOK, D], F32, kind="ExternalInput").ap()
    coefs = nc.dram_tensor("coefs", [8, TOK], F32, kind="ExternalInput").ap()
    Mh = nc.dram_tensor("Mh", [8, D, D], F32, kind="ExternalInput").ap()
    cvh = nc.dram_tensor("cvh", [8, D], F32, kind="ExternalInput").ap()
    ln2g = nc.dram_tensor("ln2g", [D], F32, kind="ExternalInput").ap()
    ln2b = nc.dram_tensor("ln2b", [D], F32, kind="ExternalInput").ap()
    ident = nc.dram_tensor("ident", [128, 128], F32, kind="ExternalInput").ap()
    outh = nc.dram_tensor("outsh", [TOK, D], F32, kind="ExternalOutput").ap()

    NT = TOK // 128
    with ExitStack() as stk:
        P = Prog(nc, stk)
        sb = lambda n, s, d=F32: stk.enter_context(nc.sbuf_tensor(n, s, d))
        ps = lambda n, s, d=F32: stk.enter_context(nc.psum_tensor(n, s, d))
        idf = sb("idf", [128, 128])
        c0 = P.emit("sp", lambda q: q.dma_start(out=idf[:, :], in_=ident[:, :]), dma=True)
        g2bc = sb("g2bc", [128, D])
        c1 = P.emit("sp", lambda q: q.dma_start(out=g2bc[:, :], in_=ln2g.partition_broadcast(128)), dma=True)
        b2bc = sb("b2bc", [128, D])
        c2 = P.emit("sp", lambda q: q.dma_start(out=b2bc[:, :], in_=ln2b.partition_broadcast(128)), dma=True)
        cvf = sb("cvf", [8, D])
        c3 = P.emit("sp", lambda q: q.dma_start(out=cvf[:, :], in_=cvh[:, :]), dma=True)
        cvb = sb("cvb", [8, D], BF16)
        c4 = P.emit("pool", lambda q: q.tensor_copy(out=cvb[:, :], in_=cvf[:, :]), deps=[c3])
        ctb = sb("ctb", [8, TOK], BF16)

        y1Tb = [sb(f"y1Tb{j}", [128, TOK], BF16) for j in range(8)]
        xt = sb("xt", [128, D])
        pF = ps("pF", [128, 128])
        xa = []
        tprev = []
        prev_e = []
        for i in range(NT):
            tl = P.emit("sp", lambda q, i=i: q.dma_start(
                out=xt[:, :], in_=y1h[i * 128:(i + 1) * 128, :]), deps=prev_e, dma=True)
            prev_e = []
            for j in range(8):
                tt = P.emit("pe", lambda q, j=j: q.transpose(
                    pF[:, :], xt[:, j * 128:(j + 1) * 128], idf[:, :]), deps=[tl, c0] + tprev)
                te = P.emit("dve", lambda q, i=i, j=j: q.tensor_copy(
                    out=y1Tb[j][:, i * 128:(i + 1) * 128], in_=pF[:, :]), deps=[tt])
                tprev = [te]
                prev_e = [te]
            xa.append(prev_e[0])

        bcb = sb("bcb", [128, TOK])
        c5 = P.emit("sp", lambda q: q.dma_start(out=bcb[0:8, :], in_=coefs[:, :]), dma=True)
        c6 = P.emit("pool", lambda q: q.tensor_copy(out=ctb[:, :], in_=bcb[0:8, :]), deps=[c5])
        accT = [sb(f"accT{j}", [128, TOK]) for j in range(8)]
        acz = [P.emit("dve", lambda q, j=j: q.memset(accT[j][:, :], 0.0)) for j in range(8)]
        pD = [ps(f"pD{u}", [128, 512]) for u in range(2)]
        pd_free = [[], []]
        dones = []
        for o in range(8):
            for tc_ in range(4):
                u = (o * 4 + tc_) % 2
                mm = P.emit("pe", lambda q, o=o, tc_=tc_: q.matmul(
                    pD[(o * 4 + tc_) % 2][:, :], cvb[:, o * 128:(o + 1) * 128],
                    ctb[:, tc_ * 512:(tc_ + 1) * 512], start=True, stop=True),
                    deps=[c4, c6] + pd_free[u])
                ad = P.emit("dve", lambda q, o=o, tc_=tc_: q.tensor_add(
                    accT[o][:, tc_ * 512:(tc_ + 1) * 512],
                    accT[o][:, tc_ * 512:(tc_ + 1) * 512], pD[(o * 4 + tc_) % 2][:, :]),
                    deps=[mm, acz[o]])
                pd_free[u] = [ad]
                dones.append(ad)

        wstage = sb("wstage", [128, 1024])
        Mb = [[sb(f"Mb{u}_{j}", [128, 1024], BF16) for j in range(8)] for u in range(2)]
        xpb = [sb(f"xpb{j}", [128, TOK], BF16) for j in range(8)]
        wfree = [[], []]
        stg_free = []
        bc_free = []
        xp_free = []
        for e in range(8):
            u = e % 2
            wt = []
            for j in range(8):
                tl = P.emit("sp", lambda q, e=e, j=j: q.dma_start(
                    out=wstage[:, :], in_=Mh[e, j * 128:(j + 1) * 128, :]),
                    deps=stg_free, dma=True)
                tc = P.emit("pool", lambda q, u=u, j=j: q.tensor_copy(
                    out=Mb[u][j][:, :], in_=wstage[:, :]), deps=[tl] + wfree[u])
                stg_free = [tc]
                wt.append(tc)
            tb = P.emit("sp", lambda q, e=e: q.dma_start(
                out=bcb[:, :], in_=coefs[e, :].partition_broadcast(128)),
                deps=bc_free + ([c6] if e == 0 else []), dma=True)
            xp = []
            for j in range(8):
                for tc_ in range(4):
                    xp.append(P.emit("dve", lambda q, j=j, tc_=tc_: q.tensor_mul(
                        xpb[j][:, tc_ * 512:(tc_ + 1) * 512],
                        y1Tb[j][:, tc_ * 512:(tc_ + 1) * 512],
                        bcb[:, tc_ * 512:(tc_ + 1) * 512]),
                        deps=[tb] + xa + xp_free))
            xp_free = []
            last = None
            for o in range(8):
                for tc_ in range(4):
                    uu = (o * 4 + tc_) % 2
                    for j in range(8):
                        dd = []
                        if j == 0:
                            dd = wt + xp + pd_free[uu]
                        last = P.emit("pe", lambda q, u=u, o=o, tc_=tc_, j=j: q.matmul(
                            pD[(o * 4 + tc_) % 2][:, :], Mb[u][j][:, o * 128:(o + 1) * 128],
                            xpb[j][:, tc_ * 512:(tc_ + 1) * 512],
                            start=(j == 0), stop=(j == 7)), deps=dd)
                    ad = P.emit("dve", lambda q, o=o, tc_=tc_: q.tensor_add(
                        accT[o][:, tc_ * 512:(tc_ + 1) * 512],
                        accT[o][:, tc_ * 512:(tc_ + 1) * 512], pD[(o * 4 + tc_) % 2][:, :]),
                        deps=[last] + (dones if e == 0 else []))
                    pd_free[uu] = [ad]
                    if e == 7:
                        dones.append(ad)
            xp_free = [last]
            bc_free = [last]
            wfree[u] = [last]
        dones = dones[-32:]

        yt = sb("yt", [128, D])
        sq2 = sb("sq2", [128, D])
        xb = sb("xb", [128, D])
        col = sb("col", [128, 1])
        col2 = sb("col2", [128, 1])
        tprev4 = tprev
        xfree = []
        yfree = []
        x3free = []
        for i in range(NT):
            tx = P.emit("sp", lambda q, i=i: q.dma_start(
                out=xt[:, :], in_=y1h[i * 128:(i + 1) * 128, :]),
                deps=(xa + dones) if i == 0 else xfree, dma=True)
            ttoks = []
            for j in range(8):
                tt = P.emit("pe", lambda q, i=i, j=j: q.transpose(
                    pF[:, :], accT[j][:, i * 128:(i + 1) * 128], idf[:, :]),
                    deps=dones + tprev4)
                te = P.emit("dve", lambda q, j=j: q.tensor_add(
                    yt[:, j * 128:(j + 1) * 128], xt[:, j * 128:(j + 1) * 128], pF[:, :]),
                    deps=[tt, tx] + yfree)
                tprev4 = [te]
                ttoks.append(te)
            xfree = ttoks
            l1 = P.emit("dve", lambda q: q.reduce_sum(col[:, :], yt[:, :], axis=AX.X), deps=ttoks)
            l2_ = P.emit("dve", lambda q: q.tensor_scalar_mul(col2[:, :], col[:, :], 1.0 / D), deps=[l1])
            l3 = P.emit("dve", lambda q: q.tensor_scalar_sub(yt[:, :], yt[:, :], col2[:, :]), deps=[l2_])
            l4 = P.emit("act", lambda q: q.square(sq2[:, :], yt[:, :]), deps=[l3])
            l5 = P.emit("dve", lambda q: q.reduce_sum(col[:, :], sq2[:, :], axis=AX.X), deps=[l4])
            l6 = P.emit("dve", lambda q: q.tensor_scalar(col2[:, :], col[:, :], 1.0 / D, 1e-5,
                                                         ALU.mult, ALU.add), deps=[l5])
            l7 = P.emit("act", lambda q: q.sqrt(col[:, :], col2[:, :]), deps=[l6])
            l8 = P.emit("dve", lambda q: q.reciprocal(col2[:, :], col[:, :]), deps=[l7])
            l9 = P.emit("dve", lambda q: q.scalar_tensor_tensor(
                yt[:, :], yt[:, :], col2[:, :], g2bc[:, :], ALU.mult, ALU.mult), deps=[l8, c1])
            l10 = P.emit("dve", lambda q: q.tensor_add(yt[:, :], yt[:, :], b2bc[:, :]), deps=[l9, c2])
            tx3 = P.emit("sp", lambda q, i=i: q.dma_start(
                out=xb[:, :], in_=x3h[i * 128:(i + 1) * 128, :]), deps=x3free, dma=True)
            f1 = P.emit("dve", lambda q: q.tensor_add(yt[:, :], yt[:, :], xb[:, :]), deps=[l10, tx3])
            x3free = [f1]
            f2 = P.emit("dve", lambda q: q.reduce_sum(col[:, :], yt[:, :], axis=AX.X), deps=[f1])
            f3 = P.emit("dve", lambda q: q.tensor_scalar_mul(col2[:, :], col[:, :], 1.0 / D), deps=[f2])
            f4 = P.emit("dve", lambda q: q.tensor_scalar_sub(yt[:, :], yt[:, :], col2[:, :]), deps=[f3])
            f5 = P.emit("act", lambda q: q.square(sq2[:, :], yt[:, :]), deps=[f4])
            f6 = P.emit("dve", lambda q: q.reduce_sum(col[:, :], sq2[:, :], axis=AX.X), deps=[f5])
            f7 = P.emit("dve", lambda q: q.tensor_scalar(col2[:, :], col[:, :], 1.0 / D, 1e-5,
                                                         ALU.mult, ALU.add), deps=[f6])
            f8 = P.emit("act", lambda q: q.sqrt(col[:, :], col2[:, :]), deps=[f7])
            f9 = P.emit("dve", lambda q: q.reciprocal(col2[:, :], col[:, :]), deps=[f8])
            f10 = P.emit("dve", lambda q: q.tensor_scalar_mul(yt[:, :], yt[:, :], col2[:, :]), deps=[f9])
            tw = P.emit("pool", lambda q, i=i: q.dma_start(
                out=outh[i * 128:(i + 1) * 128, :], in_=yt[:, :]), deps=[f10], dma=True)
            yfree = [tw, f10]

        P.barrier()
        P.finalize()
    return nc


def run_launch7c(inputs, x3, y1, gates):
    gatesG = np.ascontiguousarray(gates.reshape(B * T, 8))
    nc1 = build_launch7c1(new_nc())
    mtu = np.triu(np.ones((128, 128), np.float32), 1)
    in_maps = [{"gatesG": gatesG, "mtu": mtu} for _ in range(NCORES)]
    res = run_bass_kernel_spmd(nc1, in_maps, core_ids=list(range(NCORES)))
    coefh = res.results[0]["coefh"]
    M, cvec = run_launch7b(inputs)
    nc2 = build_launch7c2(new_nc())
    ident = _ident128()
    in_maps = []
    for c in range(NCORES):
        sl = slice(c * NPC, (c + 1) * NPC)
        y1sh = np.ascontiguousarray(y1[:, sl, :].reshape(TOK, D))
        x3sh2 = np.ascontiguousarray(x3[:, sl, :].reshape(TOK, D))
        coefsl = np.ascontiguousarray(
            coefh.reshape(4, T, 8)[:, sl, :].transpose(2, 0, 1).reshape(8, TOK))
        in_maps.append({
            "y1sh": y1sh, "x3sh2": x3sh2, "coefs": coefsl, "Mh": M, "cvh": cvec,
            "ln2g": inputs["L1_ffn_ln2_g"], "ln2b": inputs["L1_ffn_ln2_b"],
            "ident": ident,
        })
    res = run_bass_kernel_spmd(nc2, in_maps, core_ids=list(range(NCORES)))
    out = np.zeros((B, T, D), np.float32)
    for c in range(NCORES):
        out[:, c * NPC:(c + 1) * NPC, :] = res.results[c]["outsh"].reshape(B, NPC, D)
    return out



def _ident128():
    return np.eye(128, dtype=np.float32)


def run_launch1(inputs):
    x = inputs["L0_attn_x"]
    nc = build_launch1(new_nc())
    ident = _ident128()
    in_maps = []
    for c in range(NCORES):
        xs = np.ascontiguousarray(x[:, c * NPC:(c + 1) * NPC, :].reshape(TOK, D))
        in_maps.append({
            "xsh": xs,
            "wq": inputs["L0_attn_Wq"], "wk": inputs["L0_attn_Wk"],
            "wv": inputs["L0_attn_Wv"], "wb": inputs["L0_attn_Wb"],
            "lnw": inputs["L0_attn_ln_w"], "lnb": inputs["L0_attn_ln_b"],
            "ident": ident,
        })
    res = run_bass_kernel_spmd(nc, in_maps, core_ids=list(range(NCORES)))
    H, DK, DV = 4, 256, 512
    qT = np.zeros((B, H, DK, T), np.float32)
    kT = np.zeros((B, H, DK, T), np.float32)
    vT = np.zeros((B, H, DV, T), np.float32)
    bet = np.zeros((B, H, T), np.float32)
    for c in range(NCORES):
        r = res.results[c]
        sl = slice(c * NPC, (c + 1) * NPC)
        qT[:, :, :, sl] = r["qT"]
        kT[:, :, :, sl] = r["kT"]
        vT[:, :, :, sl] = r["vT"]
        bet[:, :, sl] = r["bet"]
    return qT, kT, vT, bet


def kernel(**inputs):
    qT, kT, vT, bet = run_launch1(inputs)
    o = run_launch2(qT, kT, vT, bet)
    x1 = run_launch3(inputs, o)
    x2 = run_launch4(inputs, x1)
    qT2, kT2, V2 = run_launch5(inputs, x2)
    x3 = run_launch6(inputs, x2, qT2, kT2, V2)
    y1, gates = run_launch7a(inputs, x3)
    out = run_launch7c(inputs, x3, y1, gates)
    return out
